# version 01
# baseline (speedup 1.0000x reference)
import contextlib
import numpy as np
import concourse.bass as bass
import concourse.mybir as mybir
from concourse.bass_utils import run_bass_kernel_spmd

F32 = mybir.dt.float32
F16 = mybir.dt.float16
AF = mybir.ActivationFunctionType
ALU = mybir.AluOpType
AX = mybir.AxisListType

NCORES = 8
T = 8192
C = 2048
TC = T // NCORES
NCH = C // 128
NEG = -1.0e30

TILES = {c: sorted([16 * b + c for b in range(4)] + [16 * b + 15 - c for b in range(4)])
         for c in range(NCORES)}


def core_token_idx(c):
    return np.concatenate([np.arange(t * 128, (t + 1) * 128) for t in TILES[c]])



ENGINES = ("sync", "tensor", "vector", "scalar", "gpsimd")


class Plan:

    CHAINED = ("vector", "scalar", "gpsimd")

    def __init__(self):
        self.events = []
        self.cnt = {}
        self.last_tok = {}

    def tok(self, sem):
        return (sem, self.cnt.get(sem, 0))

    def ev(self, engine, fn, waits=(), inc=None, n=1, dma=False):
        waits = list(waits)
        if engine in self.CHAINED and not dma:
            lt = self.last_tok.get(engine)
            if lt is not None:
                waits.append(lt)
            if inc is None:
                inc = "o_" + engine
                n = 1
        t = None
        if inc is not None:
            self.cnt[inc] = self.cnt.get(inc, 0) + n
            t = (inc, self.cnt[inc])
        if engine in self.CHAINED and not dma:
            self.last_tok[engine] = t
        self.events.append((engine, tuple(w for w in waits if w is not None and w[1] > 0),
                            fn, (inc, n) if inc is not None else None))
        return t

    def dma(self, engine, out, in_, sem, waits=()):
        return self.ev(engine, lambda e, o=out, i=in_: e.dma_start(out=o, in_=i),
                       waits=waits, inc=sem, n=16, dma=True)

    def check(self):
        queues = {name: [ev for ev in self.events if ev[0] == name] for name in ENGINES}
        pos = {name: 0 for name in ENGINES}
        cnt = {s: 0 for s in self.cnt}
        progress = True
        while progress:
            progress = False
            for name in ENGINES:
                while pos[name] < len(queues[name]):
                    engine, waits, fn, inc = queues[name][pos[name]]
                    if any(cnt[s] < v for (s, v) in waits):
                        break
                    if inc is not None:
                        cnt[inc[0]] += inc[1]
                    pos[name] += 1
                    progress = True
        stuck = {name: len(queues[name]) - pos[name] for name in ENGINES
                 if pos[name] < len(queues[name])}
        if stuck:
            detail = []
            for name in stuck:
                engine, waits, fn, inc = queues[name][pos[name]]
                detail.append(f"{name}@{pos[name]} waits {[(s, v, cnt[s]) for (s, v) in waits]}")
            raise RuntimeError(f"plan deadlock: remaining={stuck}; heads: {detail}")

    def emit(self, nc, stack):
        sems = {name: stack.enter_context(nc.semaphore(name)) for name in self.cnt}
        block = stack.enter_context(nc.Block())
        for name in ENGINES:
            def mk(name):
                def section(eng):
                    seen = {}
                    any_emitted = False
                    for (engine, waits, fn, inc) in self.events:
                        if engine != name:
                            continue
                        any_emitted = True
                        for (s, v) in waits:
                            if v > seen.get(s, 0):
                                eng.wait_ge(sems[s], v)
                                seen[s] = v
                        inst = fn(eng)
                        if inc is not None:
                            s, n = inc
                            if inst is None:
                                inst = eng.nop()
                            inst.then_inc(sems[s], n)
                    if not any_emitted:
                        eng.nop()
                return section
            getattr(block, name)(mk(name))


class Stream:

    def __init__(self, p, name, nslots=2):
        self.p = p
        self.name = name
        self.n = nslots
        self.i = 0

    def sem(self, slot):
        return f"{self.name}{slot}"

    def dma(self, engine, out, in_, waits=()):
        slot = self.i % self.n
        prev_full = 16 * (self.i // self.n)
        w = list(waits)
        if prev_full > 0:
            w.append((self.sem(slot), prev_full))
        self.p.dma(engine, out, in_, self.sem(slot), waits=w)
        self.i += 1
        return (self.sem(slot), prev_full + 16)

    def tok(self, idx):
        return (self.sem(idx % self.n), 16 * (idx // self.n + 1))

    def all(self):
        out = []
        for s in range(self.n):
            cnt = (self.i - s + self.n - 1) // self.n
            if cnt > 0:
                out.append((self.sem(s), 16 * cnt))
        return out




def pack_lhsT(WT):
    K, M = WT.shape
    arr = WT.reshape(K // 128, 128, M // 128, 128)
    return np.ascontiguousarray(arr.transpose(2, 1, 0, 3).reshape(M // 128, 128, K))


def rope_tables_f(idx):
    p = np.arange(64, dtype=np.float64)
    with np.errstate(under="ignore"):
        theta = np.power(10000.0, -(2.0 ** p) / 128.0)
    pos = (idx.astype(np.float64) + 1.0)[:, None]
    ang = pos * theta[None, :]
    ang = np.concatenate([ang, ang], axis=1)
    return (np.cos(ang).T.astype(np.float32).copy(),
            np.sin(ang).T.astype(np.float32).copy())


def rmat_host():
    R = np.zeros((128, 128), dtype=np.float32)
    for i in range(64):
        R[i, i + 64] = -1.0
        R[i + 64, i] = 1.0
    return np.ascontiguousarray(R.T)


def bias2d(b):
    n = b.shape[0] // 128
    return np.ascontiguousarray(b.reshape(n, 128).T)



def build_launch_A(do_rope):
    nc = bass.Bass("TRN2", target_bir_lowering=False, debug=False)
    xf = nc.dram_tensor("xf", [C, TC], F32, kind="ExternalInput").ap()
    wpk = nc.dram_tensor("wpk", [48, 128, C], F32, kind="ExternalInput").ap()
    b2 = nc.dram_tensor("b2", [128, 48], F32, kind="ExternalInput").ap()
    cosf = nc.dram_tensor("cosf", [128, TC], F32, kind="ExternalInput").ap()
    sinf = nc.dram_tensor("sinf", [128, TC], F32, kind="ExternalInput").ap()
    ones = nc.dram_tensor("ones", [128, 128], F32, kind="ExternalInput").ap()
    idn = nc.dram_tensor("idn", [128, 128], F32, kind="ExternalInput").ap()
    rmat = nc.dram_tensor("rmat", [128, 128], F32, kind="ExternalInput").ap()
    epsc = nc.dram_tensor("epsc", [128, 1], F32, kind="ExternalInput").ap()

    xn_o = nc.dram_tensor("xn", [C, TC], F32, kind="ExternalOutput").ap()
    qf_o = nc.dram_tensor("qf", [C, TC], F32, kind="ExternalOutput").ap()
    kf_o = nc.dram_tensor("kf", [C, TC], F32, kind="ExternalOutput").ap()
    vt_o = nc.dram_tensor("vt", [TC, C], F32, kind="ExternalOutput").ap()

    xf_t = xf.rearrange("(c p) t -> c p t", p=128)
    xn_t = xn_o.rearrange("(c p) t -> c p t", p=128)
    qf_t = qf_o.rearrange("(c p) t -> c p t", p=128)
    kf_t = kf_o.rearrange("(c p) t -> c p t", p=128)
    vt_byf = vt_o.rearrange("(tt p) f -> p tt f", p=128)

    with contextlib.ExitStack() as stack:
        E = stack.enter_context
        xnS = E(nc.sbuf_tensor("xnS", [128, NCH * TC], F32))
        xb = E(nc.sbuf_tensor("xb", [128, 2 * TC], F32))
        sqS = E(nc.sbuf_tensor("sqS", [128, 2 * TC], F32))
        wS = E(nc.sbuf_tensor("wS", [128, 2 * C], F32))
        cb = E(nc.sbuf_tensor("cb", [128, 2 * TC], F32))
        ro = E(nc.sbuf_tensor("ro", [128, 2 * TC], F32))
        t1 = E(nc.sbuf_tensor("t1", [128, TC], F32))
        t2 = E(nc.sbuf_tensor("t2", [128, TC], F32))
        cosS = E(nc.sbuf_tensor("cosS", [128, TC], F32))
        sinS = E(nc.sbuf_tensor("sinS", [128, TC], F32))
        onesS = E(nc.sbuf_tensor("onesS", [128, 128], F32))
        idnS = E(nc.sbuf_tensor("idnS", [128, 128], F32))
        rmS = E(nc.sbuf_tensor("rmS", [128, 128], F32))
        epsS = E(nc.sbuf_tensor("epsS", [128, 1], F32))
        b2S = E(nc.sbuf_tensor("b2S", [128, 48], F32))
        mbuf = E(nc.sbuf_tensor("mbuf", [128, TC], F32))
        u1 = E(nc.sbuf_tensor("u1", [128, TC], F32))
        u2 = E(nc.sbuf_tensor("u2", [128, TC], F32))
        u3 = E(nc.sbuf_tensor("u3", [128, TC], F32))
        Ab = E(nc.sbuf_tensor("Ab", [128, TC], F32))
        Bb = E(nc.sbuf_tensor("Bb", [128, TC], F32))
        PA = E(nc.psum_tensor("PA", [128, TC], F32))
        PB = E(nc.psum_tensor("PB", [128, TC], F32))
        PC = E(nc.psum_tensor("PC", [128, TC], F32))
        PD = E(nc.psum_tensor("PD", [128, TC], F32))

        p = Plan()
        for (dst, srcp) in ((cosS[:], cosf), (sinS[:], sinf), (onesS[:], ones),
                            (idnS[:], idn), (rmS[:], rmat), (epsS[:], epsc),
                            (b2S[:], b2)):
            p.dma("sync", dst, srcp, "s_const")
        w_const = p.tok("s_const")
        stx = Stream(p, "sxa", 2)
        stx2 = Stream(p, "sxb", 2)
        stw = Stream(p, "swa", 2)
        stxn = Stream(p, "sxn", 2)
        stqk = Stream(p, "sqk", 2)
        stvt = Stream(p, "svt", 2)

        for c in range(NCH):
            wx = stx.dma("gpsimd", xb[:, (c % 2) * TC:(c % 2 + 1) * TC], xf_t[c],
                         waits=[("s_pestat", c - 1)] if c >= 2 else [])
            p.ev("scalar", lambda e, c=c: e.activation(
                sqS[:, (c % 2) * TC:(c % 2 + 1) * TC],
                xb[:, (c % 2) * TC:(c % 2 + 1) * TC], AF.Square),
                waits=[wx] + ([("s_pestat", c - 1)] if c >= 2 else []),
                inc="s_sq")
            for h in range(2):
                p.ev("tensor", lambda e, c=c, h=h: e.matmul(
                    PA[:, h * 512:(h + 1) * 512], onesS[:],
                    xb[:, (c % 2) * TC + h * 512:(c % 2) * TC + (h + 1) * 512],
                    start=(c == 0), stop=(c == NCH - 1)),
                    waits=[wx, w_const] if h == 0 else [])
            for h in range(2):
                p.ev("tensor", lambda e, c=c, h=h: e.matmul(
                    PB[:, h * 512:(h + 1) * 512], onesS[:],
                    sqS[:, (c % 2) * TC + h * 512:(c % 2) * TC + (h + 1) * 512],
                    start=(c == 0), stop=(c == NCH - 1)),
                    waits=[("s_sq", c + 1)] if h == 0 else [],
                    inc="s_pestat" if h == 1 else None)
        w_stats = ("s_pestat", NCH)

        p.ev("vector", lambda e: e.tensor_scalar(mbuf[:], PA[:], 1.0 / C, None, ALU.mult),
             waits=[w_stats])
        p.ev("vector", lambda e: e.tensor_scalar(u1[:], PB[:], 1.0 / C, None, ALU.mult))
        p.ev("vector", lambda e: e.tensor_tensor(u2[:], mbuf[:], mbuf[:], ALU.mult))
        p.ev("vector", lambda e: e.tensor_tensor(u3[:], u1[:], u2[:], ALU.subtract),
             inc="s_var")
        p.ev("scalar", lambda e: e.activation(u1[:], u3[:], AF.Sqrt, bias=epsS[:]),
             waits=[("s_var", 1)], inc="s_std")
        p.ev("vector", lambda e: e.reciprocal(Ab[:], u1[:]), waits=[("s_std", 1)])
        p.ev("vector", lambda e: e.scalar_tensor_tensor(
            Bb[:], mbuf[:], -1.0, Ab[:], ALU.mult, ALU.mult), inc="s_ab")

        for c in range(NCH):
            wx2 = stx2.dma("gpsimd", xb[:, (c % 2) * TC:(c % 2 + 1) * TC], xf_t[c],
                           waits=([("s_ab", 1)] if c == 0 else []) +
                                 ([("s_xnc", c - 1)] if c >= 2 else []) +
                                 ([w_stats] if c < 2 else []))
            p.ev("vector", lambda e, c=c: e.tensor_tensor(
                u1[:], xb[:, (c % 2) * TC:(c % 2 + 1) * TC], Ab[:], ALU.mult),
                waits=[wx2])
            p.ev("vector", lambda e, c=c: e.tensor_tensor(
                xnS[:, c * TC:(c + 1) * TC], u1[:], Bb[:], ALU.add), inc="s_xnc")
            stxn.dma("gpsimd", xn_t[c], xnS[:, c * TC:(c + 1) * TC],
                     waits=[("s_xnc", c + 1)])
        w_xn_all = ("s_xnc", NCH)

        PROJ = (PC, PD)
        for f in range(48):
            wwf = stw.dma("sync", wS[:, (f % 2) * C:(f % 2 + 1) * C], wpk[f],
                          waits=[("s_evac", f - 1)] if f >= 2 else [])
            ps = PROJ[f % 2]
            for h in range(2):
                for kc in range(NCH):
                    is_last = (h == 1 and kc == NCH - 1)
                    p.ev("tensor", lambda e, f=f, h=h, kc=kc, ps=ps: e.matmul(
                        ps[:, h * 512:(h + 1) * 512],
                        wS[:, (f % 2) * C + kc * 128:(f % 2) * C + (kc + 1) * 128],
                        xnS[:, kc * TC + h * 512:kc * TC + (h + 1) * 512],
                        start=(kc == 0), stop=(kc == NCH - 1)),
                        waits=([wwf, w_xn_all] +
                               ([("s_evac", f - 1)] if f >= 2 else []))
                        if (h == 0 and kc == 0) else [],
                        inc="s_proj" if is_last else None)
            cfree = []
            if f >= 2:
                prev = f - 2
                if prev < 32:
                    if do_rope:
                        cfree = [("s_ropedve", prev + 1)]
                    else:
                        cfree = [stqk.tok(prev)]
                else:
                    cfree = [("s_tr", prev - 32 + 1)]
            p.ev("scalar", lambda e, f=f, ps=ps: e.activation(
                cb[:, (f % 2) * TC:(f % 2 + 1) * TC], ps[:], AF.Identity,
                bias=b2S[:, f:f + 1]),
                waits=[("s_proj", f + 1)] + cfree,
                inc="s_evac")
            w_evac = ("s_evac", f + 1)
            if f < 32 and do_rope:
                rp = (PA, PB)[f % 2]
                for h in range(2):
                    p.ev("tensor", lambda e, f=f, h=h, rp=rp: e.matmul(
                        rp[:, h * 512:(h + 1) * 512], rmS[:],
                        cb[:, (f % 2) * TC + h * 512:(f % 2) * TC + (h + 1) * 512],
                        start=True, stop=True),
                        waits=([w_evac] + ([("s_ropedve", f - 1)] if f >= 2 else []))
                        if h == 0 else [],
                        inc="s_ropemm" if h == 1 else None)
                p.ev("vector", lambda e, f=f: e.tensor_tensor(
                    t1[:], cb[:, (f % 2) * TC:(f % 2 + 1) * TC], cosS[:], ALU.mult),
                    waits=[("s_ropemm", f + 1)] +
                          ([stqk.tok(f - 2)] if f >= 2 else []))
                p.ev("vector", lambda e, f=f, rp=rp: e.tensor_tensor(
                    t2[:], rp[:], sinS[:], ALU.mult))
                p.ev("vector", lambda e, f=f: e.tensor_tensor(
                    ro[:, (f % 2) * TC:(f % 2 + 1) * TC], t1[:], t2[:], ALU.add),
                    inc="s_ropedve")
                dst = qf_t[f] if f < 16 else kf_t[f - 16]
                stqk.dma("gpsimd", dst, ro[:, (f % 2) * TC:(f % 2 + 1) * TC],
                         waits=[("s_ropedve", f + 1)])
            elif f < 32:
                dst = qf_t[f] if f < 16 else kf_t[f - 16]
                stqk.dma("gpsimd", dst, cb[:, (f % 2) * TC:(f % 2 + 1) * TC],
                         waits=[w_evac])
            else:
                fv = f - 32
                rp = (PA, PB)[f % 2]
                for tt in range(8):
                    tw = []
                    if tt == 0:
                        tw = [w_evac]
                        if fv >= 2:
                            tw.append(("s_trevac", fv - 1))
                        if do_rope and fv in (0, 1):
                            tw.append(("s_ropedve", f - 1))
                    p.ev("tensor", lambda e, f=f, tt=tt, rp=rp: e.transpose(
                        rp[:, tt * 128:(tt + 1) * 128],
                        cb[:, (f % 2) * TC + tt * 128:(f % 2) * TC + (tt + 1) * 128],
                        idnS[:]),
                        waits=tw, inc="s_tr" if tt == 7 else None)
                rovw = [("s_tr", fv + 1)]
                if fv >= 2:
                    rovw.append(stvt.tok(fv - 2))
                if do_rope and fv in (0, 1):
                    rovw.append(stqk.tok(30 + fv))
                p.ev("scalar", lambda e, f=f, rp=rp: e.copy(
                    ro[:, (f % 2) * TC:(f % 2 + 1) * TC], rp[:]),
                    waits=rovw, inc="s_trevac")
                stvt.dma("gpsimd", vt_byf[:, :, fv * 128:(fv + 1) * 128],
                         ro[:, (f % 2) * TC:(f % 2 + 1) * TC].rearrange(
                             "p (tt j) -> p tt j", j=128),
                         waits=[("s_trevac", fv + 1)])
        p.ev("gpsimd", lambda e: e.nop(),
             waits=stxn.all() + stqk.all() + stvt.all())

        p.check()
        p.emit(nc, stack)
    return nc



SCALE0 = 1.0 / float(np.sqrt(128.0))
SCALE1 = 1.0 / float(np.sqrt(32.0))
MINF = -3.0e38


def build_launch_B(layer):
    nc = bass.Bass("TRN2", target_bir_lowering=False, debug=False)
    BF16 = mybir.dt.bfloat16
    xn_i = nc.dram_tensor("xn", [C, TC], F32, kind="ExternalInput").ap()
    xf_i = nc.dram_tensor("xf", [C, TC], F32, kind="ExternalInput").ap()
    qf_i = nc.dram_tensor("qf", [C, TC], F32, kind="ExternalInput").ap()
    kfull = nc.dram_tensor("kfull", [C, T], F32, kind="ExternalInput").ap()
    vtfull = nc.dram_tensor("vtfull", [T, C], F32, kind="ExternalInput").ap()
    wot = nc.dram_tensor("wot", [C, C], F32, kind="ExternalInput").ap()
    bo2 = nc.dram_tensor("bo2", [128, 16], F32, kind="ExternalInput").ap()
    idn = nc.dram_tensor("idn", [128, 128], F32, kind="ExternalInput").ap()
    if layer == 0:
        w0pk = nc.dram_tensor("w0pk", [64, 128, C], F32, kind="ExternalInput").ap()
        b0d = nc.dram_tensor("b0d", [128, 64], F32, kind="ExternalInput").ap()
        w1t = nc.dram_tensor("w1t", [8192, C], F32, kind="ExternalInput").ap()
        b1d = nc.dram_tensor("b1d", [128, 16], F32, kind="ExternalInput").ap()
        masks = nc.dram_tensor("masks", [8, 128, 2048], BF16, kind="ExternalInput").ap()
    else:
        wgpk = nc.dram_tensor("wgpk", [64, 128, C], F32, kind="ExternalInput").ap()
        wupk = nc.dram_tensor("wupk", [64, 128, C], F32, kind="ExternalInput").ap()
        wdt = nc.dram_tensor("wdt", [8192, C], F32, kind="ExternalInput").ap()
    g_io = nc.dram_tensor("gbounce", [C, TC], F32)
    xo_o = nc.dram_tensor("xo", [C, TC], F32, kind="ExternalOutput").ap()
    import os
    DBG = os.environ.get("MK_DEBUG_B", "0") == "1"
    if DBG:
        sdbg_o = nc.dram_tensor("sdbg", [C, TC], F32, kind="ExternalOutput").ap()
        gdbg_o = nc.dram_tensor("gdbg", [C, TC], F32, kind="ExternalOutput").ap()

    xn_t = xn_i.rearrange("(c p) t -> c p t", p=128)
    xf_t = xf_i.rearrange("(c p) t -> c p t", p=128)
    qf_t = qf_i.rearrange("(c p) t -> c p t", p=128)
    g_t = g_io.ap().rearrange("(c p) t -> c p t", p=128)
    xo_t = xo_o.rearrange("(c p) t -> c p t", p=128)
    w1t_t = (w1t if layer == 0 else wdt).rearrange("(dc p) m -> dc p m", p=128)
    vt_p = vtfull.rearrange("(tt p) f -> p tt f", p=128)

    with contextlib.ExitStack() as stack:
        E = stack.enter_context
        accS = E(nc.sbuf_tensor("accS", [128, NCH * TC], F32))
        idnS = E(nc.sbuf_tensor("idnS", [128, 128], F32))
        b0S = E(nc.sbuf_tensor("b0S", [128, 64], F32))
        b1S = E(nc.sbuf_tensor("b1S", [128, 16], F32))
        boS = E(nc.sbuf_tensor("boS", [128, 16], F32))
        NSUB = 2 if layer == 1 else 1
        mS = E(nc.sbuf_tensor("mS", [128, 2 * 8 * NSUB], F32))
        lS = E(nc.sbuf_tensor("lS", [128, 2 * 8 * NSUB], F32))
        mtC = E(nc.sbuf_tensor("mtC", [128, NSUB], F32))
        dC = E(nc.sbuf_tensor("dC", [128, NSUB], F32))
        alC = E(nc.sbuf_tensor("alC", [128, NSUB], F32))
        nbC = E(nc.sbuf_tensor("nbC", [128, NSUB], F32))
        ltC = E(nc.sbuf_tensor("ltC", [128, NSUB], F32))
        rcC = E(nc.sbuf_tensor("rcC", [128, 1], F32))
        PS_A = E(nc.psum_tensor("PS_A", [128, 2048], F32))
        PS_B = E(nc.psum_tensor("PS_B", [128, 2048], F32))

        ASZ = 143360
        AB = 229344 - ASZ
        AB -= AB % 32
        assert nc.sbuf_base <= AB, f"persistent allocs {nc.sbuf_base} overlap arena {AB}"
        def arena(name, shape, dtype, off):
            return nc.alloc_sbuf_tensor_at(name, shape, dtype, offset=AB + off)
        xnS = arena("xnS", [128, NCH * TC], F32, 0)
        if layer == 0:
            w0B = arena("w0B", [128, 2 * C], F32, 65536)
            w1B = arena("w1B", [128, 4 * C], F32, 81920)
            D16 = arena("D16", [128, 2 * TC], F16, 114688)
            D32 = arena("D32", [128, 4 * TC], F32, 118784)
            gout = arena("gout", [128, 2 * TC], F32, 65536)
            gtmp = arena("gtmp", [128, TC], F32, 118784)
        else:
            wgB = arena("wgB", [128, C], F32, 65536)
            wuB = arena("wuB", [128, C], F32, 73728)
            wdB = arena("wdB", [128, 2 * C], F32, 81920)
            sgB = arena("sgB", [128, TC], F32, 98304)
            hB = arena("hB", [128, TC], F32, 102400)
            htok = arena("htok", [128, TC], F32, 106496)
            habs = arena("habs", [128, TC], F32, 110592)
            cbuf = arena("cbuf", [128, 256], F32, 114688)
            nbuf = arena("nbuf", [128, 256], F32, 115712)
            beatB = arena("beatB", [128, 4 * 256], F32, 116736)
            hsG = arena("hsG", [128, 2 * TC], F32, 120832)
            gout = arena("gout", [128, 2 * TC], F32, 129024)
        maskS = arena("maskS", [128, 8 * 2048], BF16, 0) if layer == 0 else None
        kB = arena("kB", [128, 2 * 2048], F32, 32768)
        vB = arena("vB", [128, 2 * 2048], F32, 49152)
        qB = arena("qB", [128, 2 * TC], F32, 65536)
        woB = arena("woB", [128, 2 * C], F32, 73728)
        pB = arena("pB", [128, 2 * 2048], F32, 90112)
        ptS = arena("ptS", [128, 2048], F32, 106496)
        yaccS = arena("yaccS", [128, 2 * 8 * 128], F32, 114688)
        ytS = arena("ytS", [128, TC], F32, 122880)
        yfinS = arena("yfinS", [128, 128], F32, 126976)
        xrB = arena("xrB", [128, 2 * TC], F32, 106496)
        grB = arena("grB", [128, 2 * TC], F32, 114688)
        orB = arena("orB", [128, 2 * TC], F32, 122880)

        p = Plan()
        p.dma("sync", idnS[:], idn, "s_const")
        p.dma("sync", boS[:], bo2, "s_const")
        if layer == 0:
            p.dma("sync", b0S[:], b0d, "s_const")
            p.dma("sync", b1S[:], b1d, "s_const")
        w_const = p.tok("s_const")
        stxn = Stream(p, "bxn", 2)
        stw0 = Stream(p, "bw0", 2)
        stw1 = Stream(p, "bw1", 4)
        stg = Stream(p, "bg", 2)
        stxr = Stream(p, "bxr", 2)
        stgr = Stream(p, "bgr", 2)
        sto = Stream(p, "bou", 2)

        for c in range(NCH):
            wxc = stxn.dma("gpsimd", xnS[:, c * TC:(c + 1) * TC], xn_t[c])
            if layer == 0:
                p.ev("vector", lambda e, c=c: e.tensor_copy(
                    D16[:, 0:TC], xnS[:, c * TC:(c + 1) * TC]), waits=[wxc])
                p.ev("vector", lambda e, c=c: e.tensor_copy(
                    xnS[:, c * TC:(c + 1) * TC], D16[:, 0:TC]), inc="s_rnd")
        w_xn_list = [("s_rnd", NCH)] if layer == 0 else stxn.all()

        if layer == 0:
            D_ps = (PS_A[:, 0:1024], PS_A[:, 1024:2048])
            E_ps = (PS_B[:, 0:1024], PS_B[:, 1024:2048])
            G = 4
            w1tok = {}
            for dc in range(64):
                g_i = dc // G
                w0tok = stw0.dma("sync", w0B[:, (dc % 2) * C:(dc % 2 + 1) * C],
                                 w0pk[dc],
                                 waits=[("s_devac", dc - 1)] if dc >= 2 else [])
                w1tok[dc] = stw1.dma("sync",
                                     w1B[:, (dc % 4) * C:(dc % 4 + 1) * C], w1t_t[dc],
                                     waits=[("s_eadd", 32 * g_i)]
                                     if g_i >= 1 else [])
                dps = D_ps[dc % 2]
                for h in range(2):
                    for kc in range(NCH):
                        is_last = (h == 1 and kc == NCH - 1)
                        p.ev("tensor", lambda e, dc=dc, h=h, kc=kc, dps=dps: e.matmul(
                            dps[:, h * 512:(h + 1) * 512],
                            w0B[:, (dc % 2) * C + kc * 128:(dc % 2) * C + (kc + 1) * 128],
                            xnS[:, kc * TC + h * 512:kc * TC + (h + 1) * 512],
                            start=(kc == 0), stop=(kc == NCH - 1)),
                            waits=([w0tok, w_const] + w_xn_list +
                                   ([("s_devac", dc - 1)] if dc >= 2 else []))
                            if (h == 0 and kc == 0) else [],
                            inc="s_dmm" if is_last else None)
                p.ev("scalar", lambda e, dc=dc, dps=dps: e.activation(
                    D16[:, (dc % 2) * TC:(dc % 2 + 1) * TC], dps[:], AF.Gelu,
                    bias=b0S[:, dc:dc + 1]),
                    waits=[("s_dmm", dc + 1)] + ([("s_d32", dc - 1)] if dc >= 2 else []),
                    inc="s_devac")
                p.ev("vector", lambda e, dc=dc: e.tensor_copy(
                    D32[:, (dc % 4) * TC:(dc % 4 + 1) * TC],
                    D16[:, (dc % 2) * TC:(dc % 2 + 1) * TC]),
                    waits=[("s_devac", dc + 1)] +
                          ([("s_eadd", 32 * g_i)] if g_i >= 1 else []),
                    inc="s_d32")
                if dc % G == G - 1:
                    for ec in range(NCH):
                        eps_ = E_ps[ec % 2]
                        for hh in range(2):
                            for j in range(G):
                                is_last = (j == G - 1)
                                p.ev("tensor",
                                     lambda e, ec=ec, hh=hh, j=j, g_i=g_i, eps_=eps_: e.matmul(
                                         eps_[:, hh * 512:(hh + 1) * 512],
                                         w1B[:, j * C + ec * 128:j * C + (ec + 1) * 128],
                                         D32[:, j * TC + hh * 512:j * TC + (hh + 1) * 512],
                                         start=(j == 0), stop=(j == G - 1)),
                                     waits=(([("s_d32", 4 * g_i + 4),
                                              ("s_eadd", 32 * g_i + 2 * ec - 2)] +
                                             [w1tok[4 * g_i + jj] for jj in range(G)])
                                            if (hh == 0 and j == 0) else []),
                                     inc="s_emm" if is_last else None)
                            op = ALU.add if g_i > 0 else None
                            tgt = accS[:, ec * TC + hh * 512:ec * TC + (hh + 1) * 512]
                            if g_i == 0:
                                p.ev("vector", lambda e, t=tgt, s=eps_, hh=hh: e.tensor_copy(
                                    t, s[:, hh * 512:(hh + 1) * 512]),
                                    waits=[("s_emm", 32 * g_i + 2 * ec + hh + 1)],
                                    inc="s_eadd")
                            else:
                                p.ev("vector", lambda e, t=tgt, s=eps_, hh=hh: e.tensor_tensor(
                                    t, t, s[:, hh * 512:(hh + 1) * 512], ALU.add),
                                    waits=[("s_emm", 32 * g_i + 2 * ec + hh + 1)],
                                    inc="s_eadd")
            w_ffn = ("s_eadd", 32 * 16)
            for ec in range(NCH):
                p.ev("vector", lambda e, ec=ec: e.tensor_scalar(
                    gtmp[:], accS[:, ec * TC:(ec + 1) * TC], b1S[:, ec:ec + 1],
                    None, ALU.add),
                    waits=[w_ffn] + ([stg.tok(ec - 2)] if ec >= 2 else []))
                p.ev("vector", lambda e, ec=ec: e.tensor_copy(
                    D16[:, (ec % 2) * TC:(ec % 2 + 1) * TC], gtmp[:]),
                    waits=[("s_silu", ec - 1)] if ec >= 2 else [], inc="s_g16")
                p.ev("scalar", lambda e, ec=ec: e.activation(
                    gout[:, (ec % 2) * TC:(ec % 2 + 1) * TC],
                    D16[:, (ec % 2) * TC:(ec % 2 + 1) * TC], AF.Silu),
                    waits=[("s_g16", ec + 1)], inc="s_silu")
                stg.dma("gpsimd", g_t[ec], gout[:, (ec % 2) * TC:(ec % 2 + 1) * TC],
                        waits=[("s_silu", ec + 1)])
        else:
            emit_ffn1(p, dict(xnS=xnS, wgB=wgB, wuB=wuB, wdB=wdB, sgB=sgB, hB=hB,
                              htok=htok, habs=habs, cbuf=cbuf, nbuf=nbuf,
                              beatB=beatB, hsG=hsG, gout=gout, accS=accS,
                              idnS=idnS, PS_A=PS_A, PS_B=PS_B,
                              wgpk=wgpk, wupk=wupk, wdt_t=w1t_t, g_t=g_t,
                              w_xn_list=w_xn_list, w_const=w_const, stg=stg))
        w_gdone_list = stg.all()

        if layer == 0:
            emit_attention(p, layer=0, n_units=16, hd=128, kb_el=2048, scale=SCALE0,
                           causal=True, kfull=kfull, vt_p=vt_p, qf_t=qf_t, wot=wot,
                           maskS=maskS, masks_dram=masks, w_gdone_list=w_gdone_list,
                           bufs=dict(kB=kB, vB=vB, qB=qB, woB=woB, pB=pB, ptS=ptS,
                                     ytS=ytS, yaccS=yaccS, yfinS=yfinS, idnS=idnS,
                                     mS=mS, lS=lS, mtC=mtC, dC=dC, alC=alC, nbC=nbC,
                                     ltC=ltC, rcC=rcC, accS=accS,
                                     PS_A=PS_A, PS_B=PS_B))
        else:
            emit_attention(p, layer=1, n_units=16, hd=64, kb_el=1024, scale=SCALE1,
                           causal=False, kfull=kfull, vt_p=vt_p, qf_t=qf_t, wot=wot,
                           maskS=None, masks_dram=None, w_gdone_list=w_gdone_list,
                           bufs=dict(kB=kB, vB=vB, qB=qB, woB=woB, pB=pB, ptS=ptS,
                                     ytS=ytS, yaccS=yaccS, yfinS=yfinS, idnS=idnS,
                                     mS=mS, lS=lS, mtC=mtC, dC=dC, alC=alC, nbC=nbC,
                                     ltC=ltC, rcC=rcC, accS=accS,
                                     PS_A=PS_A, PS_B=PS_B))
        w_sdone = ("s_sadd", 32 * 16)

        if DBG:
            p.dma("gpsimd", gdbg_o, g_io.ap(), "s_dbg", waits=w_gdone_list)
            sdbg_t = sdbg_o.rearrange("(c p) t -> c p t", p=128)
            for c in range(NCH):
                p.dma("gpsimd", sdbg_t[c], accS[:, c * TC:(c + 1) * TC], "s_dbg",
                      waits=[w_sdone])

        for c in range(NCH):
            wxr = stxr.dma("gpsimd", xrB[:, (c % 2) * TC:(c % 2 + 1) * TC], xf_t[c],
                           waits=[w_sdone] + ([("s_res", c - 1)] if c >= 2 else []))
            wgr = stgr.dma("sync", grB[:, (c % 2) * TC:(c % 2 + 1) * TC], g_t[c],
                           waits=[w_sdone] + ([("s_res", c - 1)] if c >= 2 else []))
            p.ev("vector", lambda e, c=c: e.tensor_scalar(
                orB[:, (c % 2) * TC:(c % 2 + 1) * TC],
                accS[:, c * TC:(c + 1) * TC], boS[:, c:c + 1], None, ALU.add),
                waits=[wxr, wgr] + ([sto.tok(c - 2)] if c >= 2 else []))
            p.ev("vector", lambda e, c=c: e.tensor_tensor(
                orB[:, (c % 2) * TC:(c % 2 + 1) * TC],
                orB[:, (c % 2) * TC:(c % 2 + 1) * TC],
                grB[:, (c % 2) * TC:(c % 2 + 1) * TC], ALU.mult))
            p.ev("vector", lambda e, c=c: e.tensor_tensor(
                orB[:, (c % 2) * TC:(c % 2 + 1) * TC],
                orB[:, (c % 2) * TC:(c % 2 + 1) * TC],
                xrB[:, (c % 2) * TC:(c % 2 + 1) * TC], ALU.add), inc="s_res")
            sto.dma("gpsimd", xo_t[c], orB[:, (c % 2) * TC:(c % 2 + 1) * TC],
                    waits=[("s_res", c + 1)])
        p.ev("gpsimd", lambda e: e.nop(),
             waits=sto.all() + ([p.tok("s_dbg")] if DBG else []))

        p.check()
        p.emit(nc, stack)
    return nc


def emit_attention(p, layer, n_units, hd, kb_el, scale, causal, kfull, vt_p, qf_t,
                   wot, maskS, masks_dram, w_gdone_list, bufs):
    b = type("B", (), bufs)
    NSUB = 2 if layer == 1 else 1
    n_kt = kb_el // 128
    NKB = 4 if causal else T // kb_el
    nkb_of = (lambda qs: qs // 2 + 1) if causal else (lambda qs: NKB)
    wot_t = wot.rearrange("(u p) m -> u p m", p=128)
    kf_t = kfull.rearrange("(u p) t -> u p t", p=128)
    n512 = kb_el // 512
    PTR = (n_kt + 7) // 8
    IPS = NSUB * PTR
    Yr = [b.PS_B[:, 1536:1536 + hd], b.PS_B[:, 1280:1280 + hd]]

    stk = Stream(p, "bk", 2)
    stv = Stream(p, "bv", 2)
    stq = Stream(p, "bq", 2)
    stwo = Stream(p, "bwo", 2)
    if causal:
        for s in range(8):
            p.dma("sync", maskS[:, s * 2048:(s + 1) * 2048], masks_dram[s],
                  "s_mask", waits=w_gdone_list if s == 0 else [])
    w_masks = p.tok("s_mask") if causal else None

    steps = []
    for u in range(n_units):
        for kb in range(NKB):
            for qs in range(8):
                if nkb_of(qs) > kb:
                    steps.append((u, kb, qs))
    last_k = {}
    last_v = {}
    last_q = {}
    for i, (u, kb, qs) in enumerate(steps):
        last_k[(u, kb)] = i
        last_v[(u, kb)] = i
        last_q[u] = i

    t = 0
    ktok = {}
    vtok = {}
    qtok = {}
    wotok = {}
    for u in range(n_units):
        wotok[u] = stwo.dma("sync", b.woB[:, (u % 2) * C:(u % 2 + 1) * C], wot_t[u],
                            waits=(w_gdone_list if u < 2
                                   else [("s_womm", 32 * (u - 1))]))
        qtok[u] = stq.dma("gpsimd", b.qB[:, (u % 2) * TC:(u % 2 + 1) * TC], qf_t[u],
                          waits=(w_gdone_list if u < 2
                                 else [("s_smm", last_q[u - 2] + 1)]))
        p.ev("vector", lambda e: e.memset(b.mS[:, 8 * NSUB:16 * NSUB], MINF),
             waits=[("s_yfin", 8 * u)] if u > 0 else [])
        p.ev("vector", lambda e: e.memset(b.lS[:, 8 * NSUB:16 * NSUB], 0.0))
        p.ev("vector", lambda e: e.memset(b.yaccS[:, 1024:2048], 0.0), inc="s_init")

        for kb in range(NKB):
            if kb >= 2:
                kw = [("s_smm", last_k[(u, kb - 2)] + 1)]
                vw = [("s_avmm", last_v[(u, kb - 2)] + 1)]
            elif u > 0:
                kw = [("s_smm", last_k[(u - 1, NKB - 2 + kb)] + 1)]
                vw = [("s_avmm", last_v[(u - 1, NKB - 2 + kb)] + 1)]
            else:
                kw = list(w_gdone_list)
                vw = list(w_gdone_list)
            wk = stk.dma("sync", b.kB[:, (kb % 2) * 2048:(kb % 2) * 2048 + kb_el],
                         kf_t[u][:, kb * kb_el:(kb + 1) * kb_el], waits=kw)
            wv = stv.dma("gpsimd",
                         b.vB[:, (kb % 2) * 2048:(kb % 2) * 2048 + kb_el].rearrange(
                             "p (tt j) -> p tt j", j=128),
                         vt_p[:, (kb * kb_el) // 128:((kb + 1) * kb_el) // 128,
                              u * 128:(u + 1) * 128],
                         waits=vw)
            for qs in range(8):
                if nkb_of(qs) <= kb:
                    continue
                pold = ((kb - 1) % 2) * 8 * NSUB
                pnew = (kb % 2) * 8 * NSUB
                for sub in range(NSUB):
                    base = sub * 64
                    nk = 128 // NSUB
                    ps_s = b.PS_A[:, sub * 1024: sub * 1024 + kb_el]
                    for i in range(n512):
                        first = (sub == 0 and i == 0)
                        w = []
                        if first:
                            w = [wk, qtok[u], ("s_init", u + 1)]
                            if t > 0:
                                w.append(("s_exp", NSUB * t))
                            if u > 0 and kb == 0 and qs == 0:
                                w.append(("s_sadd", 32 * u))
                        p.ev("tensor", lambda e, u=u, kb=kb, qs=qs, sub=sub, i=i,
                             base=base, nk=nk, ps_s=ps_s: e.matmul(
                                 ps_s[:, i * 512:(i + 1) * 512],
                                 b.qB[base:base + nk,
                                      (u % 2) * TC + qs * 128:(u % 2) * TC + (qs + 1) * 128],
                                 b.kB[base:base + nk,
                                      (kb % 2) * 2048 + i * 512:(kb % 2) * 2048 + (i + 1) * 512],
                                 start=True, stop=True,
                                 tile_position=(base, 0) if NSUB == 2 else None),
                             waits=w,
                             inc="s_smm" if (sub == NSUB - 1 and i == n512 - 1) else None)
                if causal and kb == qs // 2:
                    p.ev("vector", lambda e, qs=qs: e.tensor_tensor(
                        b.PS_A[:, 0:2048], b.PS_A[:, 0:2048],
                        maskS[:, qs * 2048:(qs + 1) * 2048], ALU.add),
                        waits=[("s_smm", t + 1), w_masks])
                for sub in range(NSUB):
                    ps_s = b.PS_A[:, sub * 1024: sub * 1024 + kb_el]
                    mo = pold + qs * NSUB + sub
                    mn = pnew + qs * NSUB + sub
                    p.ev("vector", lambda e, ps_s=ps_s, sub=sub: e.reduce_max(
                        b.mtC[:, sub:sub + 1], ps_s[:], axis=AX.X),
                        waits=[("s_smm", t + 1)])
                    p.ev("vector", lambda e, mo=mo, mn=mn, sub=sub: e.tensor_tensor(
                        b.mS[:, mn:mn + 1], b.mS[:, mo:mo + 1],
                        b.mtC[:, sub:sub + 1], ALU.max))
                    p.ev("vector", lambda e, mo=mo, mn=mn, sub=sub: e.tensor_tensor(
                        b.dC[:, sub:sub + 1], b.mS[:, mo:mo + 1],
                        b.mS[:, mn:mn + 1], ALU.subtract))
                    p.ev("vector", lambda e, mn=mn, sub=sub: e.tensor_scalar(
                        b.nbC[:, sub:sub + 1], b.mS[:, mn:mn + 1], -scale, None,
                        ALU.mult),
                        inc="s_mprep" if sub == NSUB - 1 else None)
                for sub in range(NSUB):
                    ps_s = b.PS_A[:, sub * 1024: sub * 1024 + kb_el]
                    p.ev("scalar", lambda e, sub=sub: e.activation(
                        b.alC[:, sub:sub + 1], b.dC[:, sub:sub + 1], AF.Exp,
                        scale=scale),
                        waits=[("s_mprep", t + 1)] +
                              ([("s_yupd", NSUB * t)] if t > 0 else []))
                    p.ev("scalar", lambda e, ps_s=ps_s, sub=sub, t=t: e.activation(
                        b.pB[:, (t % 2) * 2048 + sub * 1024:
                             (t % 2) * 2048 + sub * 1024 + kb_el],
                        ps_s[:], AF.Exp, bias=b.nbC[:, sub:sub + 1], scale=scale,
                        accum_out=b.ltC[:, sub:sub + 1]),
                        waits=([("s_ptmm", IPS * (t - 1))] if t >= 2 else []),
                        inc="s_exp")
                    lo = pold + qs * NSUB + sub
                    ln = pnew + qs * NSUB + sub
                    p.ev("vector", lambda e, lo=lo, ln=ln, sub=sub: e.scalar_tensor_tensor(
                        b.lS[:, ln:ln + 1], b.lS[:, lo:lo + 1], b.alC[:, sub:sub + 1],
                        b.ltC[:, sub:sub + 1], ALU.mult, ALU.add),
                        waits=[("s_exp", NSUB * t + sub + 1)])
                for sub in range(NSUB):
                    for rnd in range(PTR):
                        lo_t = rnd * 8
                        hi_t = min(n_kt, lo_t + 8)
                        for ii in range(lo_t, hi_t):
                            p.ev("tensor", lambda e, ii=ii, sub=sub, lo_t=lo_t, t=t: e.transpose(
                                b.PS_B[:, (ii - lo_t) * 128:(ii - lo_t + 1) * 128],
                                b.pB[:, (t % 2) * 2048 + sub * 1024 + ii * 128:
                                     (t % 2) * 2048 + sub * 1024 + (ii + 1) * 128],
                                b.idnS[:]),
                                waits=[("s_exp", NSUB * t + sub + 1),
                                       ("s_ptevac", p.cnt.get("s_ptevac", 0))]
                                if ii == lo_t else [],
                                inc="s_ptmm" if ii == hi_t - 1 else None)
                        p.ev("scalar", lambda e, sub=sub, lo_t=lo_t, hi_t=hi_t: e.copy(
                            b.ptS[:, sub * 1024 + lo_t * 128: sub * 1024 + hi_t * 128],
                            b.PS_B[:, 0:(hi_t - lo_t) * 128]),
                            waits=[("s_ptmm", p.cnt["s_ptmm"])], inc="s_ptevac")
                for sub in range(NSUB):
                    Y = Yr[sub]
                    for ii in range(n_kt):
                        first = (sub == 0 and ii == 0)
                        p.ev("tensor", lambda e, ii=ii, sub=sub, u=u, kb=kb, Y=Y: e.matmul(
                            Y[:],
                            b.ptS[:, sub * 1024 + ii * 128:sub * 1024 + (ii + 1) * 128],
                            b.vB[:, (kb % 2) * 2048 + ii * 128 + sub * 64:
                                 (kb % 2) * 2048 + ii * 128 + sub * 64 + hd],
                            start=(ii == 0), stop=(ii == n_kt - 1)),
                            waits=([("s_ptevac", p.cnt["s_ptevac"]), wv] +
                                   ([("s_yupd", NSUB * t)] if t > 0 else []))
                            if first else [],
                            inc="s_avmm" if (sub == NSUB - 1 and ii == n_kt - 1) else None)
                for sub in range(NSUB):
                    yo = 1024 * ((kb - 1) % 2) + qs * 128 + sub * 64
                    yn = 1024 * (kb % 2) + qs * 128 + sub * 64
                    p.ev("vector", lambda e, yo=yo, yn=yn, sub=sub, Y=Yr[sub]: e.scalar_tensor_tensor(
                        b.yaccS[:, yn:yn + hd], b.yaccS[:, yo:yo + hd],
                        b.alC[:, sub:sub + 1], Y[:], ALU.mult, ALU.add),
                        waits=[("s_avmm", t + 1)], inc="s_yupd")
                t += 1
        for qs in range(8):
            pb = ((nkb_of(qs) - 1) % 2)
            for sub in range(NSUB):
                li = pb * 8 * NSUB + qs * NSUB + sub
                p.ev("vector", lambda e, li=li: e.reciprocal(
                    b.rcC[:], b.lS[:, li:li + 1]),
                    waits=[("s_ytmm", p.cnt.get("s_ytmm", 0))])
                yslot = 1024 * pb + qs * 128 + sub * 64
                p.ev("vector", lambda e, yslot=yslot, sub=sub: e.tensor_scalar(
                    b.yfinS[:, sub * 64:sub * 64 + hd],
                    b.yaccS[:, yslot:yslot + hd], b.rcC[:], None, ALU.mult),
                    inc="s_yfin" if sub == NSUB - 1 else None)
            p.ev("tensor", lambda e: e.transpose(
                b.PS_B[:, 1024:1152], b.yfinS[:], b.idnS[:]),
                waits=[("s_yfin", p.cnt["s_yfin"]),
                       ("s_ytevac", p.cnt.get("s_ytmm", 0))],
                inc="s_ytmm")
            p.ev("scalar", lambda e, qs=qs: e.copy(
                b.ytS[:, qs * 128:(qs + 1) * 128], b.PS_B[:, 1024:1152]),
                waits=[("s_ytmm", p.cnt["s_ytmm"])], inc="s_ytevac")
        W_ps = (b.PS_A[:, 0:1024], b.PS_A[:, 1024:2048])
        for f in range(NCH):
            wps = W_ps[f % 2]
            for hh in range(2):
                pw = 32 * u + 2 * f - 2
                p.ev("tensor", lambda e, u=u, f=f, hh=hh, wps=wps: e.matmul(
                    wps[:, hh * 512:(hh + 1) * 512],
                    b.woB[:, (u % 2) * C + f * 128:(u % 2) * C + (f + 1) * 128],
                    b.ytS[:, hh * 512:(hh + 1) * 512],
                    start=True, stop=True),
                    waits=([("s_ytevac", 8 * (u + 1)), wotok[u]] +
                           ([("s_sadd", pw)] if pw > 0 else []))
                    if hh == 0 else [],
                    inc="s_womm")
                tgt = b.accS[:, f * TC + hh * 512:f * TC + (hh + 1) * 512]
                if u == 0:
                    p.ev("vector", lambda e, t_=tgt, wps=wps, hh=hh: e.tensor_copy(
                        t_, wps[:, hh * 512:(hh + 1) * 512]),
                        waits=[("s_womm", p.cnt["s_womm"])], inc="s_sadd")
                else:
                    p.ev("vector", lambda e, t_=tgt, wps=wps, hh=hh: e.tensor_tensor(
                        t_, t_, wps[:, hh * 512:(hh + 1) * 512], ALU.add),
                        waits=[("s_womm", p.cnt["s_womm"])], inc="s_sadd")


def emit_ffn1(p, b_):
    b = type("B", (), b_)
    stwg = Stream(p, "fwg", 1)
    stwu = Stream(p, "fwu", 1)
    stwd = Stream(p, "fwd", 2)
    G_ps = b.PS_A[:, 0:1024]
    U_ps = b.PS_A[:, 1024:2048]
    PT = b.PS_B[:, 0:1024]
    E_ps = b.PS_B[:, 1024:2048]
    Gg = 2
    wdtok = {}
    for ic in range(64):
        g_i = ic // Gg
        wgtok = stwg.dma("sync", b.wgB[:], b.wgpk[ic],
                         waits=[("s_gumm", ic)] if ic >= 1 else [])
        wutok = stwu.dma("sync", b.wuB[:], b.wupk[ic],
                         waits=[("s_gumm", ic)] if ic >= 1 else [])
        wdtok[ic] = stwd.dma("sync", b.wdB[:, (ic % 2) * C:(ic % 2 + 1) * C],
                             b.wdt_t[ic],
                             waits=[("s_eadd", 32 * g_i)] if g_i >= 1 else [])
        for (ps, wb, tk) in ((G_ps, b.wgB, wgtok), (U_ps, b.wuB, wutok)):
            for h in range(2):
                for kc in range(NCH):
                    is_last = (h == 1 and kc == NCH - 1)
                    p.ev("tensor", lambda e, ps=ps, wb=wb, h=h, kc=kc: e.matmul(
                        ps[:, h * 512:(h + 1) * 512],
                        wb[:, kc * 128:(kc + 1) * 128],
                        b.xnS[:, kc * TC + h * 512:kc * TC + (h + 1) * 512],
                        start=(kc == 0), stop=(kc == NCH - 1)),
                        waits=([tk, b.w_const] + b.w_xn_list +
                               ([("s_h", ic)] if ic >= 1 else []))
                        if (h == 0 and kc == 0) else [],
                        inc="s_gumm" if (is_last and ps is U_ps) else None)
        p.ev("scalar", lambda e: e.activation(b.sgB[:], G_ps[:], AF.Silu),
             waits=[("s_gumm", ic + 1)] + ([("s_h", ic)] if ic >= 1 else []),
             inc="s_sg")
        p.ev("vector", lambda e: e.tensor_tensor(b.hB[:], b.sgB[:], U_ps[:], ALU.mult),
             waits=[("s_sg", ic + 1)], inc="s_h")
        for tt in range(8):
            p.ev("tensor", lambda e, tt=tt: e.transpose(
                PT[:, tt * 128:(tt + 1) * 128],
                b.hB[:, tt * 128:(tt + 1) * 128], b.idnS[:]),
                waits=([("s_h", ic + 1), ("s_hevac", 2 * ic)] if tt == 0 else []),
                inc="s_htr" if tt == 7 else None)
        p.ev("scalar", lambda e: e.copy(b.htok[:], PT[:]),
             waits=[("s_htr", 2 * ic + 1)], inc="s_hevac")
        p.ev("scalar", lambda e: e.activation(b.habs[:], b.htok[:], AF.Abs),
             waits=[("s_hevac", 2 * ic + 1)], inc="s_habs")
        Av = [b.habs[:].rearrange("p (tt g f) -> p tt g f", g=32, f=4)[:, :, :, k]
              for k in range(4)]
        Hv = [b.htok[:].rearrange("p (tt g f) -> p tt g f", g=32, f=4)[:, :, :, k]
              for k in range(4)]
        Bt = [b.beatB[:, k * 256:(k + 1) * 256].rearrange(
              "p (tt g) -> p tt g", g=32) for k in range(4)]
        p.ev("vector", lambda e: e.memset(b.beatB[:], 0.0),
             waits=[("s_habs", ic + 1)])
        for (i, j) in ((0, 1), (0, 2), (0, 3), (1, 2), (1, 3), (2, 3)):
            p.ev("vector", lambda e, i=i, j=j: e.tensor_tensor(
                b.cbuf[:].rearrange("p (tt g) -> p tt g", g=32),
                Av[i], Av[j], ALU.is_ge))
            p.ev("vector", lambda e, i=i: e.tensor_tensor(
                Bt[i], Bt[i], b.cbuf[:].rearrange("p (tt g) -> p tt g", g=32),
                ALU.add))
            p.ev("vector", lambda e: e.tensor_scalar(
                b.nbuf[:], b.cbuf[:], -1.0, 1.0, ALU.mult, ALU.add))
            p.ev("vector", lambda e, j=j: e.tensor_tensor(
                Bt[j], Bt[j], b.nbuf[:].rearrange("p (tt g) -> p tt g", g=32),
                ALU.add))
        for k in range(4):
            p.ev("vector", lambda e, k=k: e.tensor_single_scalar(
                b.beatB[:, k * 256:(k + 1) * 256],
                b.beatB[:, k * 256:(k + 1) * 256], 1.5, ALU.is_ge))
        for k in range(4):
            p.ev("vector", lambda e, k=k: e.tensor_tensor(
                Hv[k], Hv[k], Bt[k], ALU.mult),
                inc="s_sprs" if k == 3 else None)
        for tt in range(8):
            p.ev("tensor", lambda e, tt=tt: e.transpose(
                PT[:, tt * 128:(tt + 1) * 128],
                b.htok[:, tt * 128:(tt + 1) * 128], b.idnS[:]),
                waits=([("s_sprs", ic + 1), ("s_hevac", 2 * ic + 1)]
                       if tt == 0 else []),
                inc="s_htr" if tt == 7 else None)
        p.ev("scalar", lambda e, ic=ic: e.copy(
            b.hsG[:, (ic % 2) * TC:(ic % 2 + 1) * TC], PT[:]),
            waits=[("s_htr", 2 * ic + 2)] +
                  ([("s_emm", 32 * g_i)] if g_i >= 1 else []),
            inc="s_hevac")
        if ic % Gg == Gg - 1:
            for ec in range(NCH):
                for hh in range(2):
                    for j in range(Gg):
                        is_last = (j == Gg - 1)
                        p.ev("tensor",
                             lambda e, ec=ec, hh=hh, j=j, g_i=g_i: e.matmul(
                                 E_ps[:, hh * 512:(hh + 1) * 512],
                                 b.wdB[:, j * C + ec * 128:j * C + (ec + 1) * 128],
                                 b.hsG[:, j * TC + hh * 512:j * TC + (hh + 1) * 512],
                                 start=(j == 0), stop=(j == Gg - 1)),
                             waits=(([("s_hevac", 2 * (2 * g_i + 2)),
                                      ("s_eadd", 32 * g_i + 2 * ec)] +
                                     [wdtok[2 * g_i + jj] for jj in range(Gg)])
                                    if (hh == 0 and j == 0) else []),
                             inc="s_emm" if is_last else None)
                    tgt = b.accS[:, ec * TC + hh * 512:ec * TC + (hh + 1) * 512]
                    if g_i == 0:
                        p.ev("vector", lambda e, t_=tgt, hh=hh: e.tensor_copy(
                            t_, E_ps[:, hh * 512:(hh + 1) * 512]),
                            waits=[("s_emm", 32 * g_i + 2 * ec + hh + 1)],
                            inc="s_eadd")
                    else:
                        p.ev("vector", lambda e, t_=tgt, hh=hh: e.tensor_tensor(
                            t_, t_, E_ps[:, hh * 512:(hh + 1) * 512], ALU.add),
                            waits=[("s_emm", 32 * g_i + 2 * ec + hh + 1)],
                            inc="s_eadd")
    w_ffn1 = ("s_eadd", 32 * 32)
    for ec in range(NCH):
        p.ev("scalar", lambda e, ec=ec: e.activation(
            b.gout[:, (ec % 2) * TC:(ec % 2 + 1) * TC],
            b.accS[:, ec * TC:(ec + 1) * TC], AF.Silu),
            waits=[w_ffn1] + ([b.stg.tok(ec - 2)] if ec >= 2 else []),
            inc="s_silu")
        b.stg.dma("gpsimd", b.g_t[ec],
                  b.gout[:, (ec % 2) * TC:(ec % 2 + 1) * TC],
                  waits=[("s_silu", ec + 1)])




_RUN = None


def _runner():
    return _RUN if _RUN is not None else run_bass_kernel_spmd


def host_masks(c):
    import ml_dtypes
    out = np.zeros((8, 128, 2048), dtype=np.float32)
    for si, g in enumerate(TILES[c]):
        blk = g // 16
        qpos = g * 128 + np.arange(128)[:, None]
        kpos = blk * 2048 + np.arange(2048)[None, :]
        out[si] = np.where(kpos <= qpos, 0.0, NEG)
    return out.astype(ml_dtypes.bfloat16)


def kernel(**inputs):
    run = _runner()
    x = np.ascontiguousarray(inputs["L0_attn_x"][0])
    ones = np.ones((128, 128), dtype=np.float32)
    idn = np.eye(128, dtype=np.float32)
    rmat = rmat_host()
    epsc = np.full((128, 1), 1e-5, dtype=np.float32)
    idxs = [core_token_idx(c) for c in range(NCORES)]
    ropes = [rope_tables_f(idxs[c]) for c in range(NCORES)]
    masks = [host_masks(c) for c in range(NCORES)]

    ncA0 = build_launch_A(do_rope=True)
    ncB0 = build_launch_B(0)
    ncA1 = build_launch_A(do_rope=False)
    ncB1 = build_launch_B(1)

    wpk0 = pack_lhsT(np.ascontiguousarray(inputs["L0_attn_Wqkv"].T))
    b2_0 = bias2d(inputs["L0_attn_bqkv"])
    w0pk = pack_lhsT(np.ascontiguousarray(inputs["L0_ffn_W0"].T))
    w1t = np.ascontiguousarray(inputs["L0_ffn_W1"].T)
    wot0 = np.ascontiguousarray(inputs["L0_attn_Wo"].T)
    wqkv1 = np.concatenate([inputs["L1_attn_Wq"], inputs["L1_attn_Wk"],
                            inputs["L1_attn_Wv"]], axis=1)
    wpk1 = pack_lhsT(np.ascontiguousarray(wqkv1))
    b2_1 = bias2d(np.concatenate([inputs["L1_attn_bq"], inputs["L1_attn_bk"],
                                  inputs["L1_attn_bv"]]))
    wot1 = np.ascontiguousarray(inputs["L1_attn_Wo"])
    wgpk = pack_lhsT(np.ascontiguousarray(inputs["L1_ffn_w_gate"]))
    wupk = pack_lhsT(np.ascontiguousarray(inputs["L1_ffn_w_up"]))
    wdt = np.ascontiguousarray(inputs["L1_ffn_w_down"])

    def launchA(nc, x_slices, wpk, b2, rope_on):
        in_maps = []
        for c in range(NCORES):
            cosf, sinf = ropes[c] if rope_on else (np.zeros((128, TC), np.float32),
                                                   np.zeros((128, TC), np.float32))
            in_maps.append(dict(xf=x_slices[c], wpk=wpk, b2=b2, cosf=cosf,
                                sinf=sinf, ones=ones, idn=idn, rmat=rmat,
                                epsc=epsc))
        res = run(nc, in_maps, core_ids=list(range(NCORES)))
        kfull = np.zeros((C, T), dtype=np.float32)
        vtfull = np.zeros((T, C), dtype=np.float32)
        for c in range(NCORES):
            kfull[:, idxs[c]] = res.results[c]["kf"]
            vtfull[idxs[c], :] = res.results[c]["vt"]
        return res, np.ascontiguousarray(kfull), np.ascontiguousarray(vtfull)

    x_slices = [np.ascontiguousarray(x[idxs[c]].T) for c in range(NCORES)]

    resA0, kfull0, vt0 = launchA(ncA0, x_slices, wpk0, b2_0, True)
    in_mapsB0 = []
    for c in range(NCORES):
        in_mapsB0.append(dict(
            xn=resA0.results[c]["xn"], xf=x_slices[c], qf=resA0.results[c]["qf"],
            kfull=kfull0, vtfull=vt0, wot=wot0, bo2=bias2d(inputs["L0_attn_bo"]),
            idn=idn, w0pk=w0pk, b0d=bias2d(inputs["L0_ffn_bias0"]),
            w1t=w1t, b1d=bias2d(inputs["L0_ffn_bias1"]), masks=masks[c]))
    resB0 = run(ncB0, in_mapsB0, core_ids=list(range(NCORES)))
    x1_slices = [np.ascontiguousarray(resB0.results[c]["xo"]) for c in range(NCORES)]

    resA1, kfull1, vt1 = launchA(ncA1, x1_slices, wpk1, b2_1, False)
    in_mapsB1 = []
    for c in range(NCORES):
        in_mapsB1.append(dict(
            xn=resA1.results[c]["xn"], xf=x1_slices[c], qf=resA1.results[c]["qf"],
            kfull=kfull1, vtfull=vt1, wot=wot1, bo2=bias2d(inputs["L1_attn_bo"]),
            idn=idn, wgpk=wgpk, wupk=wupk, wdt=wdt))
    resB1 = run(ncB1, in_mapsB1, core_ids=list(range(NCORES)))

    out = np.zeros((T, C), dtype=np.float32)
    for c in range(NCORES):
        out[idxs[c], :] = resB1.results[c]["xo"].T
    return out[None]


# version 02
# speedup vs baseline: 1.0300x; 1.0300x over previous
import contextlib
import numpy as np
import concourse.bass as bass
import concourse.mybir as mybir
from concourse.bass_utils import run_bass_kernel_spmd

F32 = mybir.dt.float32
F16 = mybir.dt.float16
AF = mybir.ActivationFunctionType
ALU = mybir.AluOpType
AX = mybir.AxisListType

NCORES = 8
T = 8192
C = 2048
TC = T // NCORES
NCH = C // 128
NEG = -1.0e30

TILES = {c: sorted([16 * b + c for b in range(4)] + [16 * b + 15 - c for b in range(4)])
         for c in range(NCORES)}


def core_token_idx(c):
    return np.concatenate([np.arange(t * 128, (t + 1) * 128) for t in TILES[c]])



ENGINES = ("sync", "tensor", "vector", "scalar", "gpsimd")


class Plan:

    CHAINED = ("vector", "scalar", "gpsimd")

    def __init__(self):
        self.events = []
        self.cnt = {}
        self.last_tok = {}

    def tok(self, sem):
        return (sem, self.cnt.get(sem, 0))

    def ev(self, engine, fn, waits=(), inc=None, n=1, dma=False):
        waits = list(waits)
        if engine in self.CHAINED and not dma:
            lt = self.last_tok.get(engine)
            if lt is not None:
                waits.append(lt)
            if inc is None:
                inc = "o_" + engine
                n = 1
        t = None
        if inc is not None:
            self.cnt[inc] = self.cnt.get(inc, 0) + n
            t = (inc, self.cnt[inc])
        if engine in self.CHAINED and not dma:
            self.last_tok[engine] = t
        self.events.append((engine, tuple(w for w in waits if w is not None and w[1] > 0),
                            fn, (inc, n) if inc is not None else None))
        return t

    def dma(self, engine, out, in_, sem, waits=()):
        return self.ev(engine, lambda e, o=out, i=in_: e.dma_start(out=o, in_=i),
                       waits=waits, inc=sem, n=16, dma=True)

    def check(self):
        queues = {name: [ev for ev in self.events if ev[0] == name] for name in ENGINES}
        pos = {name: 0 for name in ENGINES}
        cnt = {s: 0 for s in self.cnt}
        progress = True
        while progress:
            progress = False
            for name in ENGINES:
                while pos[name] < len(queues[name]):
                    engine, waits, fn, inc = queues[name][pos[name]]
                    if any(cnt[s] < v for (s, v) in waits):
                        break
                    if inc is not None:
                        cnt[inc[0]] += inc[1]
                    pos[name] += 1
                    progress = True
        stuck = {name: len(queues[name]) - pos[name] for name in ENGINES
                 if pos[name] < len(queues[name])}
        if stuck:
            detail = []
            for name in stuck:
                engine, waits, fn, inc = queues[name][pos[name]]
                detail.append(f"{name}@{pos[name]} waits {[(s, v, cnt[s]) for (s, v) in waits]}")
            raise RuntimeError(f"plan deadlock: remaining={stuck}; heads: {detail}")

    def emit(self, nc, stack):
        sems = {name: stack.enter_context(nc.semaphore(name)) for name in self.cnt}
        block = stack.enter_context(nc.Block())
        for name in ENGINES:
            def mk(name):
                def section(eng):
                    seen = {}
                    any_emitted = False
                    for (engine, waits, fn, inc) in self.events:
                        if engine != name:
                            continue
                        any_emitted = True
                        for (s, v) in waits:
                            if v > seen.get(s, 0):
                                eng.wait_ge(sems[s], v)
                                seen[s] = v
                        inst = fn(eng)
                        if inc is not None:
                            s, n = inc
                            if inst is None:
                                inst = eng.nop()
                            inst.then_inc(sems[s], n)
                    if not any_emitted:
                        eng.nop()
                return section
            getattr(block, name)(mk(name))


class Stream:

    def __init__(self, p, name, nslots=2):
        self.p = p
        self.name = name
        self.n = nslots
        self.i = 0

    def sem(self, slot):
        return f"{self.name}{slot}"

    def dma(self, engine, out, in_, waits=()):
        slot = self.i % self.n
        prev_full = 16 * (self.i // self.n)
        w = list(waits)
        if prev_full > 0:
            w.append((self.sem(slot), prev_full))
        self.p.dma(engine, out, in_, self.sem(slot), waits=w)
        self.i += 1
        return (self.sem(slot), prev_full + 16)

    def tok(self, idx):
        return (self.sem(idx % self.n), 16 * (idx // self.n + 1))

    def all(self):
        out = []
        for s in range(self.n):
            cnt = (self.i - s + self.n - 1) // self.n
            if cnt > 0:
                out.append((self.sem(s), 16 * cnt))
        return out




def pack_lhsT(WT):
    K, M = WT.shape
    arr = WT.reshape(K // 128, 128, M // 128, 128)
    return np.ascontiguousarray(arr.transpose(2, 1, 0, 3).reshape(M // 128, 128, K))


def rope_tables_f(idx):
    p = np.arange(64, dtype=np.float64)
    with np.errstate(under="ignore"):
        theta = np.power(10000.0, -(2.0 ** p) / 128.0)
    pos = (idx.astype(np.float64) + 1.0)[:, None]
    ang = pos * theta[None, :]
    ang = np.concatenate([ang, ang], axis=1)
    return (np.cos(ang).T.astype(np.float32).copy(),
            np.sin(ang).T.astype(np.float32).copy())


def rmat_host():
    R = np.zeros((128, 128), dtype=np.float32)
    for i in range(64):
        R[i, i + 64] = -1.0
        R[i + 64, i] = 1.0
    return np.ascontiguousarray(R.T)


def bias2d(b):
    n = b.shape[0] // 128
    return np.ascontiguousarray(b.reshape(n, 128).T)



def build_launch_A(do_rope):
    nc = bass.Bass("TRN2", target_bir_lowering=False, debug=False)
    xf = nc.dram_tensor("xf", [C, TC], F32, kind="ExternalInput").ap()
    wpk = nc.dram_tensor("wpk", [48, 128, C], F32, kind="ExternalInput").ap()
    b2 = nc.dram_tensor("b2", [128, 48], F32, kind="ExternalInput").ap()
    cosf = nc.dram_tensor("cosf", [128, TC], F32, kind="ExternalInput").ap()
    sinf = nc.dram_tensor("sinf", [128, TC], F32, kind="ExternalInput").ap()
    ones = nc.dram_tensor("ones", [128, 128], F32, kind="ExternalInput").ap()
    idn = nc.dram_tensor("idn", [128, 128], F32, kind="ExternalInput").ap()
    rmat = nc.dram_tensor("rmat", [128, 128], F32, kind="ExternalInput").ap()
    epsc = nc.dram_tensor("epsc", [128, 1], F32, kind="ExternalInput").ap()

    xn_o = nc.dram_tensor("xn", [C, TC], F32, kind="ExternalOutput").ap()
    qf_o = nc.dram_tensor("qf", [C, TC], F32, kind="ExternalOutput").ap()
    kf_o = nc.dram_tensor("kf", [C, TC], F32, kind="ExternalOutput").ap()
    vt_o = nc.dram_tensor("vt", [TC, C], F32, kind="ExternalOutput").ap()

    xf_t = xf.rearrange("(c p) t -> c p t", p=128)
    xn_t = xn_o.rearrange("(c p) t -> c p t", p=128)
    qf_t = qf_o.rearrange("(c p) t -> c p t", p=128)
    kf_t = kf_o.rearrange("(c p) t -> c p t", p=128)
    vt_byf = vt_o.rearrange("(tt p) f -> p tt f", p=128)

    with contextlib.ExitStack() as stack:
        E = stack.enter_context
        xnS = E(nc.sbuf_tensor("xnS", [128, NCH * TC], F32))
        xb = E(nc.sbuf_tensor("xb", [128, 2 * TC], F32))
        sqS = E(nc.sbuf_tensor("sqS", [128, 2 * TC], F32))
        wS = E(nc.sbuf_tensor("wS", [128, 2 * C], F32))
        cb = E(nc.sbuf_tensor("cb", [128, 2 * TC], F32))
        ro = E(nc.sbuf_tensor("ro", [128, 2 * TC], F32))
        t1 = E(nc.sbuf_tensor("t1", [128, TC], F32))
        t2 = E(nc.sbuf_tensor("t2", [128, TC], F32))
        cosS = E(nc.sbuf_tensor("cosS", [128, TC], F32))
        sinS = E(nc.sbuf_tensor("sinS", [128, TC], F32))
        onesS = E(nc.sbuf_tensor("onesS", [128, 128], F32))
        idnS = E(nc.sbuf_tensor("idnS", [128, 128], F32))
        rmS = E(nc.sbuf_tensor("rmS", [128, 128], F32))
        epsS = E(nc.sbuf_tensor("epsS", [128, 1], F32))
        b2S = E(nc.sbuf_tensor("b2S", [128, 48], F32))
        mbuf = E(nc.sbuf_tensor("mbuf", [128, TC], F32))
        u1 = E(nc.sbuf_tensor("u1", [128, TC], F32))
        u2 = E(nc.sbuf_tensor("u2", [128, TC], F32))
        u3 = E(nc.sbuf_tensor("u3", [128, TC], F32))
        Ab = E(nc.sbuf_tensor("Ab", [128, TC], F32))
        Bb = E(nc.sbuf_tensor("Bb", [128, TC], F32))
        PA = E(nc.psum_tensor("PA", [128, TC], F32))
        PB = E(nc.psum_tensor("PB", [128, TC], F32))
        PC = E(nc.psum_tensor("PC", [128, TC], F32))
        PD = E(nc.psum_tensor("PD", [128, TC], F32))

        p = Plan()
        for (dst, srcp) in ((cosS[:], cosf), (sinS[:], sinf), (onesS[:], ones),
                            (idnS[:], idn), (rmS[:], rmat), (epsS[:], epsc),
                            (b2S[:], b2)):
            p.dma("sync", dst, srcp, "s_const")
        w_const = p.tok("s_const")
        stx = Stream(p, "sxa", 2)
        stx2 = Stream(p, "sxb", 2)
        stw = Stream(p, "swa", 2)
        stxn = Stream(p, "sxn", 2)
        stqk = Stream(p, "sqk", 2)
        stvt = Stream(p, "svt", 2)

        for c in range(NCH):
            wx = stx.dma("gpsimd", xb[:, (c % 2) * TC:(c % 2 + 1) * TC], xf_t[c],
                         waits=[("s_pestat", c - 1)] if c >= 2 else [])
            p.ev("scalar", lambda e, c=c: e.activation(
                sqS[:, (c % 2) * TC:(c % 2 + 1) * TC],
                xb[:, (c % 2) * TC:(c % 2 + 1) * TC], AF.Square),
                waits=[wx] + ([("s_pestat", c - 1)] if c >= 2 else []),
                inc="s_sq")
            for h in range(2):
                p.ev("tensor", lambda e, c=c, h=h: e.matmul(
                    PA[:, h * 512:(h + 1) * 512], onesS[:],
                    xb[:, (c % 2) * TC + h * 512:(c % 2) * TC + (h + 1) * 512],
                    start=(c == 0), stop=(c == NCH - 1)),
                    waits=[wx, w_const] if h == 0 else [])
            for h in range(2):
                p.ev("tensor", lambda e, c=c, h=h: e.matmul(
                    PB[:, h * 512:(h + 1) * 512], onesS[:],
                    sqS[:, (c % 2) * TC + h * 512:(c % 2) * TC + (h + 1) * 512],
                    start=(c == 0), stop=(c == NCH - 1)),
                    waits=[("s_sq", c + 1)] if h == 0 else [],
                    inc="s_pestat" if h == 1 else None)
        w_stats = ("s_pestat", NCH)

        p.ev("vector", lambda e: e.tensor_scalar(mbuf[:], PA[:], 1.0 / C, None, ALU.mult),
             waits=[w_stats])
        p.ev("vector", lambda e: e.tensor_scalar(u1[:], PB[:], 1.0 / C, None, ALU.mult))
        p.ev("vector", lambda e: e.tensor_tensor(u2[:], mbuf[:], mbuf[:], ALU.mult))
        p.ev("vector", lambda e: e.tensor_tensor(u3[:], u1[:], u2[:], ALU.subtract),
             inc="s_var")
        p.ev("scalar", lambda e: e.activation(u1[:], u3[:], AF.Sqrt, bias=epsS[:]),
             waits=[("s_var", 1)], inc="s_std")
        p.ev("vector", lambda e: e.reciprocal(Ab[:], u1[:]), waits=[("s_std", 1)])
        p.ev("vector", lambda e: e.scalar_tensor_tensor(
            Bb[:], mbuf[:], -1.0, Ab[:], ALU.mult, ALU.mult), inc="s_ab")

        for c in range(NCH):
            wx2 = stx2.dma("gpsimd", xb[:, (c % 2) * TC:(c % 2 + 1) * TC], xf_t[c],
                           waits=([("s_ab", 1)] if c == 0 else []) +
                                 ([("s_xnc", c - 1)] if c >= 2 else []) +
                                 ([w_stats] if c < 2 else []))
            p.ev("vector", lambda e, c=c: e.tensor_tensor(
                u1[:], xb[:, (c % 2) * TC:(c % 2 + 1) * TC], Ab[:], ALU.mult),
                waits=[wx2])
            p.ev("vector", lambda e, c=c: e.tensor_tensor(
                xnS[:, c * TC:(c + 1) * TC], u1[:], Bb[:], ALU.add), inc="s_xnc")
            stxn.dma("gpsimd", xn_t[c], xnS[:, c * TC:(c + 1) * TC],
                     waits=[("s_xnc", c + 1)])
        w_xn_all = ("s_xnc", NCH)

        PROJ = (PC, PD)
        for f in range(48):
            wwf = stw.dma("sync", wS[:, (f % 2) * C:(f % 2 + 1) * C], wpk[f],
                          waits=[("s_evac", f - 1)] if f >= 2 else [])
            ps = PROJ[f % 2]
            for h in range(2):
                for kc in range(NCH):
                    is_last = (h == 1 and kc == NCH - 1)
                    p.ev("tensor", lambda e, f=f, h=h, kc=kc, ps=ps: e.matmul(
                        ps[:, h * 512:(h + 1) * 512],
                        wS[:, (f % 2) * C + kc * 128:(f % 2) * C + (kc + 1) * 128],
                        xnS[:, kc * TC + h * 512:kc * TC + (h + 1) * 512],
                        start=(kc == 0), stop=(kc == NCH - 1)),
                        waits=([wwf, w_xn_all] +
                               ([("s_evac", f - 1)] if f >= 2 else []))
                        if (h == 0 and kc == 0) else [],
                        inc="s_proj" if is_last else None)
            cfree = []
            if f >= 2:
                prev = f - 2
                if prev < 32:
                    if do_rope:
                        cfree = [("s_ropedve", prev + 1)]
                    else:
                        cfree = [stqk.tok(prev)]
                else:
                    cfree = [("s_tr", prev - 32 + 1)]
            p.ev("scalar", lambda e, f=f, ps=ps: e.activation(
                cb[:, (f % 2) * TC:(f % 2 + 1) * TC], ps[:], AF.Identity,
                bias=b2S[:, f:f + 1]),
                waits=[("s_proj", f + 1)] + cfree,
                inc="s_evac")
            w_evac = ("s_evac", f + 1)
            if f < 32 and do_rope:
                rp = (PA, PB)[f % 2]
                for h in range(2):
                    p.ev("tensor", lambda e, f=f, h=h, rp=rp: e.matmul(
                        rp[:, h * 512:(h + 1) * 512], rmS[:],
                        cb[:, (f % 2) * TC + h * 512:(f % 2) * TC + (h + 1) * 512],
                        start=True, stop=True),
                        waits=([w_evac] + ([("s_ropedve", f - 1)] if f >= 2 else []))
                        if h == 0 else [],
                        inc="s_ropemm" if h == 1 else None)
                p.ev("vector", lambda e, f=f: e.tensor_tensor(
                    t1[:], cb[:, (f % 2) * TC:(f % 2 + 1) * TC], cosS[:], ALU.mult),
                    waits=[("s_ropemm", f + 1)] +
                          ([stqk.tok(f - 2)] if f >= 2 else []))
                p.ev("vector", lambda e, f=f, rp=rp: e.tensor_tensor(
                    t2[:], rp[:], sinS[:], ALU.mult))
                p.ev("vector", lambda e, f=f: e.tensor_tensor(
                    ro[:, (f % 2) * TC:(f % 2 + 1) * TC], t1[:], t2[:], ALU.add),
                    inc="s_ropedve")
                dst = qf_t[f] if f < 16 else kf_t[f - 16]
                stqk.dma("gpsimd", dst, ro[:, (f % 2) * TC:(f % 2 + 1) * TC],
                         waits=[("s_ropedve", f + 1)])
            elif f < 32:
                dst = qf_t[f] if f < 16 else kf_t[f - 16]
                stqk.dma("gpsimd", dst, cb[:, (f % 2) * TC:(f % 2 + 1) * TC],
                         waits=[w_evac])
            else:
                fv = f - 32
                rp = (PA, PB)[f % 2]
                for tt in range(8):
                    tw = []
                    if tt == 0:
                        tw = [w_evac]
                        if fv >= 2:
                            tw.append(("s_trevac", fv - 1))
                        if do_rope and fv in (0, 1):
                            tw.append(("s_ropedve", f - 1))
                    p.ev("tensor", lambda e, f=f, tt=tt, rp=rp: e.transpose(
                        rp[:, tt * 128:(tt + 1) * 128],
                        cb[:, (f % 2) * TC + tt * 128:(f % 2) * TC + (tt + 1) * 128],
                        idnS[:]),
                        waits=tw, inc="s_tr" if tt == 7 else None)
                rovw = [("s_tr", fv + 1)]
                if fv >= 2:
                    rovw.append(stvt.tok(fv - 2))
                if do_rope and fv in (0, 1):
                    rovw.append(stqk.tok(30 + fv))
                p.ev("scalar", lambda e, f=f, rp=rp: e.copy(
                    ro[:, (f % 2) * TC:(f % 2 + 1) * TC], rp[:]),
                    waits=rovw, inc="s_trevac")
                stvt.dma("gpsimd", vt_byf[:, :, fv * 128:(fv + 1) * 128],
                         ro[:, (f % 2) * TC:(f % 2 + 1) * TC].rearrange(
                             "p (tt j) -> p tt j", j=128),
                         waits=[("s_trevac", fv + 1)])
        p.ev("gpsimd", lambda e: e.nop(),
             waits=stxn.all() + stqk.all() + stvt.all())

        p.check()
        p.emit(nc, stack)
    return nc



SCALE0 = 1.0 / float(np.sqrt(128.0))
SCALE1 = 1.0 / float(np.sqrt(32.0))
MINF = -3.0e38


def build_launch_B(layer):
    nc = bass.Bass("TRN2", target_bir_lowering=False, debug=False)
    BF16 = mybir.dt.bfloat16
    xn_i = nc.dram_tensor("xn", [C, TC], F32, kind="ExternalInput").ap()
    xf_i = nc.dram_tensor("xf", [C, TC], F32, kind="ExternalInput").ap()
    qf_i = nc.dram_tensor("qf", [C, TC], F32, kind="ExternalInput").ap()
    kfull = nc.dram_tensor("kfull", [C, T], F32, kind="ExternalInput").ap()
    vtfull = nc.dram_tensor("vtfull", [T, C], F32, kind="ExternalInput").ap()
    wot = nc.dram_tensor("wot", [C, C], F32, kind="ExternalInput").ap()
    bo2 = nc.dram_tensor("bo2", [128, 16], F32, kind="ExternalInput").ap()
    idn = nc.dram_tensor("idn", [128, 128], F32, kind="ExternalInput").ap()
    if layer == 0:
        w0pk = nc.dram_tensor("w0pk", [64, 128, C], F32, kind="ExternalInput").ap()
        b0d = nc.dram_tensor("b0d", [128, 64], F32, kind="ExternalInput").ap()
        w1t = nc.dram_tensor("w1t", [8192, C], F32, kind="ExternalInput").ap()
        b1d = nc.dram_tensor("b1d", [128, 16], F32, kind="ExternalInput").ap()
        masks = nc.dram_tensor("masks", [8, 128, 2048], BF16, kind="ExternalInput").ap()
    else:
        wgpk = nc.dram_tensor("wgpk", [64, 128, C], F32, kind="ExternalInput").ap()
        wupk = nc.dram_tensor("wupk", [64, 128, C], F32, kind="ExternalInput").ap()
        wdt = nc.dram_tensor("wdt", [8192, C], F32, kind="ExternalInput").ap()
    g_io = nc.dram_tensor("gbounce", [C, TC], F32)
    xo_o = nc.dram_tensor("xo", [C, TC], F32, kind="ExternalOutput").ap()
    import os
    DBG = os.environ.get("MK_DEBUG_B", "0") == "1"
    if DBG:
        sdbg_o = nc.dram_tensor("sdbg", [C, TC], F32, kind="ExternalOutput").ap()
        gdbg_o = nc.dram_tensor("gdbg", [C, TC], F32, kind="ExternalOutput").ap()

    xn_t = xn_i.rearrange("(c p) t -> c p t", p=128)
    xf_t = xf_i.rearrange("(c p) t -> c p t", p=128)
    qf_t = qf_i.rearrange("(c p) t -> c p t", p=128)
    g_t = g_io.ap().rearrange("(c p) t -> c p t", p=128)
    xo_t = xo_o.rearrange("(c p) t -> c p t", p=128)
    w1t_t = (w1t if layer == 0 else wdt).rearrange("(dc p) m -> dc p m", p=128)
    vt_p = vtfull.rearrange("(tt p) f -> p tt f", p=128)

    with contextlib.ExitStack() as stack:
        E = stack.enter_context
        accS = E(nc.sbuf_tensor("accS", [128, NCH * TC], F32))
        idnS = E(nc.sbuf_tensor("idnS", [128, 128], F32))
        b0S = E(nc.sbuf_tensor("b0S", [128, 64], F32))
        b1S = E(nc.sbuf_tensor("b1S", [128, 16], F32))
        boS = E(nc.sbuf_tensor("boS", [128, 16], F32))
        NSUB = 2 if layer == 1 else 1
        mS = E(nc.sbuf_tensor("mS", [128, 2 * 8 * NSUB], F32))
        lS = E(nc.sbuf_tensor("lS", [128, 2 * 8 * NSUB], F32))
        mtC = E(nc.sbuf_tensor("mtC", [128, NSUB], F32))
        dC = E(nc.sbuf_tensor("dC", [128, NSUB], F32))
        alC = E(nc.sbuf_tensor("alC", [128, 2 * NSUB], F32))
        nbC = E(nc.sbuf_tensor("nbC", [128, NSUB], F32))
        ltC = E(nc.sbuf_tensor("ltC", [128, NSUB], F32))
        rcC = E(nc.sbuf_tensor("rcC", [128, 1], F32))
        PS_A = E(nc.psum_tensor("PS_A", [128, 2048], F32))
        PS_B = E(nc.psum_tensor("PS_B", [128, 2048], F32))

        ASZ = 143360
        AB = 229344 - ASZ
        AB -= AB % 32
        assert nc.sbuf_base <= AB, f"persistent allocs {nc.sbuf_base} overlap arena {AB}"
        def arena(name, shape, dtype, off):
            return nc.alloc_sbuf_tensor_at(name, shape, dtype, offset=AB + off)
        xnS = arena("xnS", [128, NCH * TC], F32, 0)
        if layer == 0:
            w0B = arena("w0B", [128, 2 * C], F32, 65536)
            w1B = arena("w1B", [128, 4 * C], F32, 81920)
            D16 = arena("D16", [128, 2 * TC], F16, 114688)
            D32 = arena("D32", [128, 4 * TC], F32, 118784)
            gout = arena("gout", [128, 2 * TC], F32, 65536)
            gtmp = arena("gtmp", [128, TC], F32, 118784)
        else:
            wgB = arena("wgB", [128, C], F32, 65536)
            wuB = arena("wuB", [128, C], F32, 73728)
            wdB = arena("wdB", [128, 2 * C], F32, 81920)
            sgB = arena("sgB", [128, TC], F32, 98304)
            hB = arena("hB", [128, TC], F32, 102400)
            htok = arena("htok", [128, TC], F32, 106496)
            habs = arena("habs", [128, TC], F32, 110592)
            cbuf = arena("cbuf", [128, 256], F32, 114688)
            nbuf = arena("nbuf", [128, 256], F32, 115712)
            beatB = arena("beatB", [128, 4 * 256], F32, 116736)
            hsG = arena("hsG", [128, 2 * TC], F32, 120832)
            gout = arena("gout", [128, 2 * TC], F32, 129024)
        maskS = arena("maskS", [128, 8 * 2048], BF16, 0) if layer == 0 else None
        kB = arena("kB", [128, 2 * 2048], F32, 32768)
        vB = arena("vB", [128, 2 * 2048], F32, 49152)
        qB = arena("qB", [128, 2 * TC], F32, 65536)
        woB = arena("woB", [128, 2 * C], F32, 73728)
        pB = arena("pB", [128, 2 * 2048], F32, 90112)
        ptS = arena("ptS", [128, 2048], F32, 106496)
        yaccS = arena("yaccS", [128, 2 * 8 * 128], F32, 114688)
        ytS = arena("ytS", [128, TC], F32, 122880)
        yfinS = arena("yfinS", [128, 128], F32, 126976)
        xrB = arena("xrB", [128, 2 * TC], F32, 106496)
        grB = arena("grB", [128, 2 * TC], F32, 114688)
        orB = arena("orB", [128, 2 * TC], F32, 122880)

        p = Plan()
        p.dma("sync", idnS[:], idn, "s_const")
        p.dma("sync", boS[:], bo2, "s_const")
        if layer == 0:
            p.dma("sync", b0S[:], b0d, "s_const")
            p.dma("sync", b1S[:], b1d, "s_const")
        w_const = p.tok("s_const")
        stxn = Stream(p, "bxn", 2)
        stw0 = Stream(p, "bw0", 2)
        stw1 = Stream(p, "bw1", 4)
        stg = Stream(p, "bg", 2)
        stxr = Stream(p, "bxr", 2)
        stgr = Stream(p, "bgr", 2)
        sto = Stream(p, "bou", 2)

        for c in range(NCH):
            wxc = stxn.dma("gpsimd", xnS[:, c * TC:(c + 1) * TC], xn_t[c])
            if layer == 0:
                p.ev("vector", lambda e, c=c: e.tensor_copy(
                    D16[:, 0:TC], xnS[:, c * TC:(c + 1) * TC]), waits=[wxc])
                p.ev("vector", lambda e, c=c: e.tensor_copy(
                    xnS[:, c * TC:(c + 1) * TC], D16[:, 0:TC]), inc="s_rnd")
        w_xn_list = [("s_rnd", NCH)] if layer == 0 else stxn.all()

        if layer == 0:
            D_ps = (PS_A[:, 0:1024], PS_A[:, 1024:2048])
            E_ps = (PS_B[:, 0:1024], PS_B[:, 1024:2048])
            G = 4
            w1tok = {}
            for dc in range(64):
                g_i = dc // G
                w0tok = stw0.dma("sync", w0B[:, (dc % 2) * C:(dc % 2 + 1) * C],
                                 w0pk[dc],
                                 waits=[("s_devac", dc - 1)] if dc >= 2 else [])
                w1tok[dc] = stw1.dma("sync",
                                     w1B[:, (dc % 4) * C:(dc % 4 + 1) * C], w1t_t[dc],
                                     waits=[("s_eadd", 32 * g_i)]
                                     if g_i >= 1 else [])
                dps = D_ps[dc % 2]
                for h in range(2):
                    for kc in range(NCH):
                        is_last = (h == 1 and kc == NCH - 1)
                        p.ev("tensor", lambda e, dc=dc, h=h, kc=kc, dps=dps: e.matmul(
                            dps[:, h * 512:(h + 1) * 512],
                            w0B[:, (dc % 2) * C + kc * 128:(dc % 2) * C + (kc + 1) * 128],
                            xnS[:, kc * TC + h * 512:kc * TC + (h + 1) * 512],
                            start=(kc == 0), stop=(kc == NCH - 1)),
                            waits=([w0tok, w_const] + w_xn_list +
                                   ([("s_devac", dc - 1)] if dc >= 2 else []))
                            if (h == 0 and kc == 0) else [],
                            inc="s_dmm" if is_last else None)
                p.ev("scalar", lambda e, dc=dc, dps=dps: e.activation(
                    D16[:, (dc % 2) * TC:(dc % 2 + 1) * TC], dps[:], AF.Gelu,
                    bias=b0S[:, dc:dc + 1]),
                    waits=[("s_dmm", dc + 1)] + ([("s_d32", dc - 1)] if dc >= 2 else []),
                    inc="s_devac")
                p.ev("vector", lambda e, dc=dc: e.tensor_copy(
                    D32[:, (dc % 4) * TC:(dc % 4 + 1) * TC],
                    D16[:, (dc % 2) * TC:(dc % 2 + 1) * TC]),
                    waits=[("s_devac", dc + 1)] +
                          ([("s_eadd", 32 * g_i)] if g_i >= 1 else []),
                    inc="s_d32")
                if dc % G == G - 1:
                    for ec in range(NCH):
                        eps_ = E_ps[ec % 2]
                        for hh in range(2):
                            for j in range(G):
                                is_last = (j == G - 1)
                                p.ev("tensor",
                                     lambda e, ec=ec, hh=hh, j=j, g_i=g_i, eps_=eps_: e.matmul(
                                         eps_[:, hh * 512:(hh + 1) * 512],
                                         w1B[:, j * C + ec * 128:j * C + (ec + 1) * 128],
                                         D32[:, j * TC + hh * 512:j * TC + (hh + 1) * 512],
                                         start=(j == 0), stop=(j == G - 1)),
                                     waits=(([("s_d32", 4 * g_i + 4),
                                              ("s_eadd", 32 * g_i + 2 * ec - 2)] +
                                             [w1tok[4 * g_i + jj] for jj in range(G)])
                                            if (hh == 0 and j == 0) else []),
                                     inc="s_emm" if is_last else None)
                            op = ALU.add if g_i > 0 else None
                            tgt = accS[:, ec * TC + hh * 512:ec * TC + (hh + 1) * 512]
                            if g_i == 0:
                                p.ev("vector", lambda e, t=tgt, s=eps_, hh=hh: e.tensor_copy(
                                    t, s[:, hh * 512:(hh + 1) * 512]),
                                    waits=[("s_emm", 32 * g_i + 2 * ec + hh + 1)],
                                    inc="s_eadd")
                            else:
                                p.ev("vector", lambda e, t=tgt, s=eps_, hh=hh: e.tensor_tensor(
                                    t, t, s[:, hh * 512:(hh + 1) * 512], ALU.add),
                                    waits=[("s_emm", 32 * g_i + 2 * ec + hh + 1)],
                                    inc="s_eadd")
            w_ffn = ("s_eadd", 32 * 16)
            for ec in range(NCH):
                p.ev("vector", lambda e, ec=ec: e.tensor_scalar(
                    gtmp[:], accS[:, ec * TC:(ec + 1) * TC], b1S[:, ec:ec + 1],
                    None, ALU.add),
                    waits=[w_ffn] + ([stg.tok(ec - 2)] if ec >= 2 else []))
                p.ev("vector", lambda e, ec=ec: e.tensor_copy(
                    D16[:, (ec % 2) * TC:(ec % 2 + 1) * TC], gtmp[:]),
                    waits=[("s_silu", ec - 1)] if ec >= 2 else [], inc="s_g16")
                p.ev("scalar", lambda e, ec=ec: e.activation(
                    gout[:, (ec % 2) * TC:(ec % 2 + 1) * TC],
                    D16[:, (ec % 2) * TC:(ec % 2 + 1) * TC], AF.Silu),
                    waits=[("s_g16", ec + 1)], inc="s_silu")
                stg.dma("gpsimd", g_t[ec], gout[:, (ec % 2) * TC:(ec % 2 + 1) * TC],
                        waits=[("s_silu", ec + 1)])
        else:
            emit_ffn1(p, dict(xnS=xnS, wgB=wgB, wuB=wuB, wdB=wdB, sgB=sgB, hB=hB,
                              htok=htok, habs=habs, cbuf=cbuf, nbuf=nbuf,
                              beatB=beatB, hsG=hsG, gout=gout, accS=accS,
                              idnS=idnS, PS_A=PS_A, PS_B=PS_B,
                              wgpk=wgpk, wupk=wupk, wdt_t=w1t_t, g_t=g_t,
                              w_xn_list=w_xn_list, w_const=w_const, stg=stg))
        w_gdone_list = stg.all()

        if layer == 0:
            emit_attention(p, layer=0, n_units=16, hd=128, kb_el=2048, scale=SCALE0,
                           causal=True, kfull=kfull, vt_p=vt_p, qf_t=qf_t, wot=wot,
                           maskS=maskS, masks_dram=masks, w_gdone_list=w_gdone_list,
                           bufs=dict(kB=kB, vB=vB, qB=qB, woB=woB, pB=pB, ptS=ptS,
                                     ytS=ytS, yaccS=yaccS, yfinS=yfinS, idnS=idnS,
                                     mS=mS, lS=lS, mtC=mtC, dC=dC, alC=alC, nbC=nbC,
                                     ltC=ltC, rcC=rcC, accS=accS,
                                     PS_A=PS_A, PS_B=PS_B))
        else:
            emit_attention(p, layer=1, n_units=16, hd=64, kb_el=1024, scale=SCALE1,
                           causal=False, kfull=kfull, vt_p=vt_p, qf_t=qf_t, wot=wot,
                           maskS=None, masks_dram=None, w_gdone_list=w_gdone_list,
                           bufs=dict(kB=kB, vB=vB, qB=qB, woB=woB, pB=pB, ptS=ptS,
                                     ytS=ytS, yaccS=yaccS, yfinS=yfinS, idnS=idnS,
                                     mS=mS, lS=lS, mtC=mtC, dC=dC, alC=alC, nbC=nbC,
                                     ltC=ltC, rcC=rcC, accS=accS,
                                     PS_A=PS_A, PS_B=PS_B))
        w_sdone = ("s_sadd", 32 * 16)

        if DBG:
            p.dma("gpsimd", gdbg_o, g_io.ap(), "s_dbg", waits=w_gdone_list)
            sdbg_t = sdbg_o.rearrange("(c p) t -> c p t", p=128)
            for c in range(NCH):
                p.dma("gpsimd", sdbg_t[c], accS[:, c * TC:(c + 1) * TC], "s_dbg",
                      waits=[w_sdone])

        for c in range(NCH):
            wxr = stxr.dma("gpsimd", xrB[:, (c % 2) * TC:(c % 2 + 1) * TC], xf_t[c],
                           waits=[w_sdone] + ([("s_res", c - 1)] if c >= 2 else []))
            wgr = stgr.dma("sync", grB[:, (c % 2) * TC:(c % 2 + 1) * TC], g_t[c],
                           waits=[w_sdone] + ([("s_res", c - 1)] if c >= 2 else []))
            p.ev("vector", lambda e, c=c: e.tensor_scalar(
                orB[:, (c % 2) * TC:(c % 2 + 1) * TC],
                accS[:, c * TC:(c + 1) * TC], boS[:, c:c + 1], None, ALU.add),
                waits=[wxr, wgr] + ([sto.tok(c - 2)] if c >= 2 else []))
            p.ev("vector", lambda e, c=c: e.tensor_tensor(
                orB[:, (c % 2) * TC:(c % 2 + 1) * TC],
                orB[:, (c % 2) * TC:(c % 2 + 1) * TC],
                grB[:, (c % 2) * TC:(c % 2 + 1) * TC], ALU.mult))
            p.ev("vector", lambda e, c=c: e.tensor_tensor(
                orB[:, (c % 2) * TC:(c % 2 + 1) * TC],
                orB[:, (c % 2) * TC:(c % 2 + 1) * TC],
                xrB[:, (c % 2) * TC:(c % 2 + 1) * TC], ALU.add), inc="s_res")
            sto.dma("gpsimd", xo_t[c], orB[:, (c % 2) * TC:(c % 2 + 1) * TC],
                    waits=[("s_res", c + 1)])
        p.ev("gpsimd", lambda e: e.nop(),
             waits=sto.all() + ([p.tok("s_dbg")] if DBG else []))

        p.check()
        p.emit(nc, stack)
    return nc


def emit_attention(p, layer, n_units, hd, kb_el, scale, causal, kfull, vt_p, qf_t,
                   wot, maskS, masks_dram, w_gdone_list, bufs):
    b = type("B", (), bufs)
    NSUB = 2 if layer == 1 else 1
    n_kt = kb_el // 128
    NKB = 4 if causal else T // kb_el
    nkb_of = (lambda qs: qs // 2 + 1) if causal else (lambda qs: NKB)
    wot_t = wot.rearrange("(u p) m -> u p m", p=128)
    kf_t = kfull.rearrange("(u p) t -> u p t", p=128)
    n512 = kb_el // 512
    PTR = (n_kt + 7) // 8
    Yr = [b.PS_B[:, 1536:1536 + hd], b.PS_B[:, 1280:1280 + hd]]

    stk = Stream(p, "bk", 2)
    stv = Stream(p, "bv", 2)
    stq = Stream(p, "bq", 2)
    stwo = Stream(p, "bwo", 2)
    if causal:
        for s in range(8):
            p.dma("sync", maskS[:, s * 2048:(s + 1) * 2048], masks_dram[s],
                  "s_mask", waits=w_gdone_list if s == 0 else [])
    w_masks = p.tok("s_mask") if causal else None

    steps = []
    for u in range(n_units):
        for kb in range(NKB):
            for qs in range(8):
                if nkb_of(qs) > kb:
                    steps.append((u, kb, qs))
    last_k = {}
    last_v = {}
    last_q = {}
    for i, (u, kb, qs) in enumerate(steps):
        last_k[(u, kb)] = i
        last_v[(u, kb)] = i
        last_q[u] = i

    exp_tok = {}
    ptmm_tok = {}
    ptevac_tok = {}
    avmm_tok = {}
    yupd_tok = {}
    pending = {}

    def phase2(t, u, kb, qs, wv):
        pold = ((kb - 1) % 2) * 8 * NSUB
        pnew = (kb % 2) * 8 * NSUB
        for sub in range(NSUB):
            for rnd in range(PTR):
                lo_t = rnd * 8
                hi_t = min(n_kt, lo_t + 8)
                for ii in range(lo_t, hi_t):
                    tw = []
                    if ii == lo_t:
                        tw = [exp_tok[t]]
                        pv = ptevac_tok.get(t, p.tok("s_ptevac"))
                        tw.append(p.tok("s_ptevac"))
                    tk = p.ev("tensor", lambda e, ii=ii, sub=sub, lo_t=lo_t, t=t:
                              e.transpose(
                                  b.PS_B[:, (ii - lo_t) * 128:(ii - lo_t + 1) * 128],
                                  b.pB[:, (t % 2) * 2048 + sub * 1024 + ii * 128:
                                       (t % 2) * 2048 + sub * 1024 + (ii + 1) * 128],
                                  b.idnS[:]),
                              waits=tw, inc="s_ptmm" if ii == hi_t - 1 else None)
                    if ii == hi_t - 1:
                        ptmm_tok[t] = tk
                p.ev("scalar", lambda e, sub=sub, lo_t=lo_t, hi_t=hi_t: e.copy(
                    b.ptS[:, sub * 1024 + lo_t * 128: sub * 1024 + hi_t * 128],
                    b.PS_B[:, 0:(hi_t - lo_t) * 128]),
                    waits=[ptmm_tok[t]], inc="s_ptevac")
                ptevac_tok[t] = p.tok("s_ptevac")
        for sub in range(NSUB):
            Y = Yr[sub]
            for ii in range(n_kt):
                first = (sub == 0 and ii == 0)
                w = []
                if first:
                    w = [ptevac_tok[t], wv]
                    if t > 0:
                        w.append(yupd_tok[t - 1])
                tk = p.ev("tensor", lambda e, ii=ii, sub=sub, kb=kb, Y=Y: e.matmul(
                    Y[:],
                    b.ptS[:, sub * 1024 + ii * 128:sub * 1024 + (ii + 1) * 128],
                    b.vB[:, (kb % 2) * 2048 + ii * 128 + sub * 64:
                         (kb % 2) * 2048 + ii * 128 + sub * 64 + hd],
                    start=(ii == 0), stop=(ii == n_kt - 1)),
                    waits=w,
                    inc="s_avmm" if (sub == NSUB - 1 and ii == n_kt - 1) else None)
                if sub == NSUB - 1 and ii == n_kt - 1:
                    avmm_tok[t] = tk
        for sub in range(NSUB):
            yo = 1024 * ((kb - 1) % 2) + qs * 128 + sub * 64
            yn = 1024 * (kb % 2) + qs * 128 + sub * 64
            tk = p.ev("vector", lambda e, yo=yo, yn=yn, sub=sub, t=t:
                      e.scalar_tensor_tensor(
                          b.yaccS[:, yn:yn + hd], b.yaccS[:, yo:yo + hd],
                          b.alC[:, (t % 2) * NSUB + sub:(t % 2) * NSUB + sub + 1],
                          Yr[sub][:], ALU.mult, ALU.add),
                      waits=[avmm_tok[t]], inc="s_yupd")
            yupd_tok[t] = tk

    t = 0
    ktok = {}
    vtok = {}
    qtok = {}
    wotok = {}
    for u in range(n_units):
        wotok[u] = stwo.dma("sync", b.woB[:, (u % 2) * C:(u % 2 + 1) * C], wot_t[u],
                            waits=(w_gdone_list if u < 2
                                   else [("s_womm", 32 * (u - 1))]))
        qtok[u] = stq.dma("gpsimd", b.qB[:, (u % 2) * TC:(u % 2 + 1) * TC], qf_t[u],
                          waits=(w_gdone_list if u < 2
                                 else [("s_smm", last_q[u - 2] + 1)]))
        p.ev("vector", lambda e: e.memset(b.mS[:, 8 * NSUB:16 * NSUB], MINF),
             waits=[("s_yfin", 8 * u)] if u > 0 else [])
        p.ev("vector", lambda e: e.memset(b.lS[:, 8 * NSUB:16 * NSUB], 0.0))
        p.ev("vector", lambda e: e.memset(b.yaccS[:, 1024:2048], 0.0), inc="s_init")

        for kb in range(NKB):
            if kb >= 2:
                kw = [("s_smm", last_k[(u, kb - 2)] + 1)]
                vw = [("s_avmm", last_v[(u, kb - 2)] + 1)]
            elif u > 0:
                kw = [("s_smm", last_k[(u - 1, NKB - 2 + kb)] + 1)]
                vw = [("s_avmm", last_v[(u - 1, NKB - 2 + kb)] + 1)]
            else:
                kw = list(w_gdone_list)
                vw = list(w_gdone_list)
            wk = stk.dma("sync", b.kB[:, (kb % 2) * 2048:(kb % 2) * 2048 + kb_el],
                         kf_t[u][:, kb * kb_el:(kb + 1) * kb_el], waits=kw)
            wv = stv.dma("gpsimd",
                         b.vB[:, (kb % 2) * 2048:(kb % 2) * 2048 + kb_el].rearrange(
                             "p (tt j) -> p tt j", j=128),
                         vt_p[:, (kb * kb_el) // 128:((kb + 1) * kb_el) // 128,
                              u * 128:(u + 1) * 128],
                         waits=vw)
            for qs in range(8):
                if nkb_of(qs) <= kb:
                    continue
                pold = ((kb - 1) % 2) * 8 * NSUB
                pnew = (kb % 2) * 8 * NSUB
                for sub in range(NSUB):
                    base = sub * 64
                    nk = 128 // NSUB
                    ps_s = b.PS_A[:, sub * 1024: sub * 1024 + kb_el]
                    for i in range(n512):
                        first = (sub == 0 and i == 0)
                        w = []
                        if first:
                            w = [wk, qtok[u], ("s_init", u + 1)]
                            if t > 0:
                                w.append(exp_tok[t - 1])
                            if u > 0 and kb == 0 and qs == 0:
                                w.append(("s_sadd", 32 * u))
                        p.ev("tensor", lambda e, u=u, kb=kb, qs=qs, sub=sub, i=i,
                             base=base, nk=nk, ps_s=ps_s: e.matmul(
                                 ps_s[:, i * 512:(i + 1) * 512],
                                 b.qB[base:base + nk,
                                      (u % 2) * TC + qs * 128:(u % 2) * TC + (qs + 1) * 128],
                                 b.kB[base:base + nk,
                                      (kb % 2) * 2048 + i * 512:(kb % 2) * 2048 + (i + 1) * 512],
                                 start=True, stop=True,
                                 tile_position=(base, 0) if NSUB == 2 else None),
                             waits=w,
                             inc="s_smm" if (sub == NSUB - 1 and i == n512 - 1) else None)
                if causal and kb == qs // 2:
                    p.ev("vector", lambda e, qs=qs: e.tensor_tensor(
                        b.PS_A[:, 0:2048], b.PS_A[:, 0:2048],
                        maskS[:, qs * 2048:(qs + 1) * 2048], ALU.add),
                        waits=[("s_smm", t + 1), w_masks])
                for sub in range(NSUB):
                    ps_s = b.PS_A[:, sub * 1024: sub * 1024 + kb_el]
                    mo = pold + qs * NSUB + sub
                    mn = pnew + qs * NSUB + sub
                    p.ev("vector", lambda e, ps_s=ps_s, sub=sub: e.reduce_max(
                        b.mtC[:, sub:sub + 1], ps_s[:], axis=AX.X),
                        waits=[("s_smm", t + 1)])
                    p.ev("vector", lambda e, mo=mo, mn=mn, sub=sub: e.tensor_tensor(
                        b.mS[:, mn:mn + 1], b.mS[:, mo:mo + 1],
                        b.mtC[:, sub:sub + 1], ALU.max))
                    p.ev("vector", lambda e, mo=mo, mn=mn, sub=sub: e.tensor_tensor(
                        b.dC[:, sub:sub + 1], b.mS[:, mo:mo + 1],
                        b.mS[:, mn:mn + 1], ALU.subtract))
                    p.ev("vector", lambda e, mn=mn, sub=sub: e.tensor_scalar(
                        b.nbC[:, sub:sub + 1], b.mS[:, mn:mn + 1], -scale, None,
                        ALU.mult),
                        inc="s_mprep" if sub == NSUB - 1 else None)
                for sub in range(NSUB):
                    ps_s = b.PS_A[:, sub * 1024: sub * 1024 + kb_el]
                    p.ev("scalar", lambda e, sub=sub, t=t: e.activation(
                        b.alC[:, (t % 2) * NSUB + sub:(t % 2) * NSUB + sub + 1],
                        b.dC[:, sub:sub + 1], AF.Exp, scale=scale),
                        waits=[("s_mprep", t + 1)] +
                              ([yupd_tok[t - 2]] if t >= 2 else []))
                    tk = p.ev("scalar", lambda e, ps_s=ps_s, sub=sub, t=t: e.activation(
                        b.pB[:, (t % 2) * 2048 + sub * 1024:
                             (t % 2) * 2048 + sub * 1024 + kb_el],
                        ps_s[:], AF.Exp, bias=b.nbC[:, sub:sub + 1], scale=scale,
                        accum_out=b.ltC[:, sub:sub + 1]),
                        waits=([ptmm_tok[t - 2]] if t >= 2 else []),
                        inc="s_exp")
                    exp_tok[t] = tk
                    lo = pold + qs * NSUB + sub
                    ln = pnew + qs * NSUB + sub
                    p.ev("vector", lambda e, lo=lo, ln=ln, sub=sub, t=t:
                         e.scalar_tensor_tensor(
                             b.lS[:, ln:ln + 1], b.lS[:, lo:lo + 1],
                             b.alC[:, (t % 2) * NSUB + sub:(t % 2) * NSUB + sub + 1],
                             b.ltC[:, sub:sub + 1], ALU.mult, ALU.add),
                         waits=[exp_tok[t]])
                if (t - 1) in pending:
                    pu, pkb, pqs, pwv = pending[t - 1]
                    phase2(t - 1, pu, pkb, pqs, pwv)
                    del pending[t - 1]
                pending[t] = (u, kb, qs, wv)
                t += 1
        if (t - 1) in pending:
            pu, pkb, pqs, pwv = pending[t - 1]
            phase2(t - 1, pu, pkb, pqs, pwv)
            del pending[t - 1]
        for qs in range(8):
            pb = ((nkb_of(qs) - 1) % 2)
            for sub in range(NSUB):
                li = pb * 8 * NSUB + qs * NSUB + sub
                p.ev("vector", lambda e, li=li: e.reciprocal(
                    b.rcC[:], b.lS[:, li:li + 1]),
                    waits=[("s_ytmm", p.cnt.get("s_ytmm", 0))])
                yslot = 1024 * pb + qs * 128 + sub * 64
                p.ev("vector", lambda e, yslot=yslot, sub=sub: e.tensor_scalar(
                    b.yfinS[:, sub * 64:sub * 64 + hd],
                    b.yaccS[:, yslot:yslot + hd], b.rcC[:], None, ALU.mult),
                    inc="s_yfin" if sub == NSUB - 1 else None)
            p.ev("tensor", lambda e: e.transpose(
                b.PS_B[:, 1024:1152], b.yfinS[:], b.idnS[:]),
                waits=[("s_yfin", p.cnt["s_yfin"]),
                       ("s_ytevac", p.cnt.get("s_ytmm", 0))],
                inc="s_ytmm")
            p.ev("scalar", lambda e, qs=qs: e.copy(
                b.ytS[:, qs * 128:(qs + 1) * 128], b.PS_B[:, 1024:1152]),
                waits=[("s_ytmm", p.cnt["s_ytmm"])], inc="s_ytevac")
        W_ps = (b.PS_A[:, 0:1024], b.PS_A[:, 1024:2048])
        for f in range(NCH):
            wps = W_ps[f % 2]
            for hh in range(2):
                pw = 32 * u + 2 * f - 2
                p.ev("tensor", lambda e, u=u, f=f, hh=hh, wps=wps: e.matmul(
                    wps[:, hh * 512:(hh + 1) * 512],
                    b.woB[:, (u % 2) * C + f * 128:(u % 2) * C + (f + 1) * 128],
                    b.ytS[:, hh * 512:(hh + 1) * 512],
                    start=True, stop=True),
                    waits=([("s_ytevac", 8 * (u + 1)), wotok[u]] +
                           ([("s_sadd", pw)] if pw > 0 else []))
                    if hh == 0 else [],
                    inc="s_womm")
                tgt = b.accS[:, f * TC + hh * 512:f * TC + (hh + 1) * 512]
                if u == 0:
                    p.ev("vector", lambda e, t_=tgt, wps=wps, hh=hh: e.tensor_copy(
                        t_, wps[:, hh * 512:(hh + 1) * 512]),
                        waits=[("s_womm", p.cnt["s_womm"])], inc="s_sadd")
                else:
                    p.ev("vector", lambda e, t_=tgt, wps=wps, hh=hh: e.tensor_tensor(
                        t_, t_, wps[:, hh * 512:(hh + 1) * 512], ALU.add),
                        waits=[("s_womm", p.cnt["s_womm"])], inc="s_sadd")


def emit_ffn1(p, b_):
    b = type("B", (), b_)
    stwg = Stream(p, "fwg", 1)
    stwu = Stream(p, "fwu", 1)
    stwd = Stream(p, "fwd", 2)
    G_ps = b.PS_A[:, 0:1024]
    U_ps = b.PS_A[:, 1024:2048]
    PT = b.PS_B[:, 0:1024]
    E_ps = b.PS_B[:, 1024:2048]
    Gg = 2
    wdtok = {}
    for ic in range(64):
        g_i = ic // Gg
        wgtok = stwg.dma("sync", b.wgB[:], b.wgpk[ic],
                         waits=[("s_gumm", ic)] if ic >= 1 else [])
        wutok = stwu.dma("sync", b.wuB[:], b.wupk[ic],
                         waits=[("s_gumm", ic)] if ic >= 1 else [])
        wdtok[ic] = stwd.dma("sync", b.wdB[:, (ic % 2) * C:(ic % 2 + 1) * C],
                             b.wdt_t[ic],
                             waits=[("s_eadd", 32 * g_i)] if g_i >= 1 else [])
        for (ps, wb, tk) in ((G_ps, b.wgB, wgtok), (U_ps, b.wuB, wutok)):
            for h in range(2):
                for kc in range(NCH):
                    is_last = (h == 1 and kc == NCH - 1)
                    p.ev("tensor", lambda e, ps=ps, wb=wb, h=h, kc=kc: e.matmul(
                        ps[:, h * 512:(h + 1) * 512],
                        wb[:, kc * 128:(kc + 1) * 128],
                        b.xnS[:, kc * TC + h * 512:kc * TC + (h + 1) * 512],
                        start=(kc == 0), stop=(kc == NCH - 1)),
                        waits=([tk, b.w_const] + b.w_xn_list +
                               ([("s_h", ic)] if ic >= 1 else []))
                        if (h == 0 and kc == 0) else [],
                        inc="s_gumm" if (is_last and ps is U_ps) else None)
        p.ev("scalar", lambda e: e.activation(b.sgB[:], G_ps[:], AF.Silu),
             waits=[("s_gumm", ic + 1)] + ([("s_h", ic)] if ic >= 1 else []),
             inc="s_sg")
        p.ev("vector", lambda e: e.tensor_tensor(b.hB[:], b.sgB[:], U_ps[:], ALU.mult),
             waits=[("s_sg", ic + 1)], inc="s_h")
        for tt in range(8):
            p.ev("tensor", lambda e, tt=tt: e.transpose(
                PT[:, tt * 128:(tt + 1) * 128],
                b.hB[:, tt * 128:(tt + 1) * 128], b.idnS[:]),
                waits=([("s_h", ic + 1), ("s_hevac", 2 * ic)] if tt == 0 else []),
                inc="s_htr" if tt == 7 else None)
        p.ev("scalar", lambda e: e.copy(b.htok[:], PT[:]),
             waits=[("s_htr", 2 * ic + 1)], inc="s_hevac")
        p.ev("scalar", lambda e: e.activation(b.habs[:], b.htok[:], AF.Abs),
             waits=[("s_hevac", 2 * ic + 1)], inc="s_habs")
        Av = [b.habs[:].rearrange("p (tt g f) -> p tt g f", g=32, f=4)[:, :, :, k]
              for k in range(4)]
        Hv = [b.htok[:].rearrange("p (tt g f) -> p tt g f", g=32, f=4)[:, :, :, k]
              for k in range(4)]
        Bt = [b.beatB[:, k * 256:(k + 1) * 256].rearrange(
              "p (tt g) -> p tt g", g=32) for k in range(4)]
        p.ev("vector", lambda e: e.memset(b.beatB[:], 0.0),
             waits=[("s_habs", ic + 1)])
        for (i, j) in ((0, 1), (0, 2), (0, 3), (1, 2), (1, 3), (2, 3)):
            p.ev("vector", lambda e, i=i, j=j: e.tensor_tensor(
                b.cbuf[:].rearrange("p (tt g) -> p tt g", g=32),
                Av[i], Av[j], ALU.is_ge))
            p.ev("vector", lambda e, i=i: e.tensor_tensor(
                Bt[i], Bt[i], b.cbuf[:].rearrange("p (tt g) -> p tt g", g=32),
                ALU.add))
            p.ev("vector", lambda e: e.tensor_scalar(
                b.nbuf[:], b.cbuf[:], -1.0, 1.0, ALU.mult, ALU.add))
            p.ev("vector", lambda e, j=j: e.tensor_tensor(
                Bt[j], Bt[j], b.nbuf[:].rearrange("p (tt g) -> p tt g", g=32),
                ALU.add))
        for k in range(4):
            p.ev("vector", lambda e, k=k: e.tensor_single_scalar(
                b.beatB[:, k * 256:(k + 1) * 256],
                b.beatB[:, k * 256:(k + 1) * 256], 1.5, ALU.is_ge))
        for k in range(4):
            p.ev("vector", lambda e, k=k: e.tensor_tensor(
                Hv[k], Hv[k], Bt[k], ALU.mult),
                inc="s_sprs" if k == 3 else None)
        for tt in range(8):
            p.ev("tensor", lambda e, tt=tt: e.transpose(
                PT[:, tt * 128:(tt + 1) * 128],
                b.htok[:, tt * 128:(tt + 1) * 128], b.idnS[:]),
                waits=([("s_sprs", ic + 1), ("s_hevac", 2 * ic + 1)]
                       if tt == 0 else []),
                inc="s_htr" if tt == 7 else None)
        p.ev("scalar", lambda e, ic=ic: e.copy(
            b.hsG[:, (ic % 2) * TC:(ic % 2 + 1) * TC], PT[:]),
            waits=[("s_htr", 2 * ic + 2)] +
                  ([("s_emm", 32 * g_i)] if g_i >= 1 else []),
            inc="s_hevac")
        if ic % Gg == Gg - 1:
            for ec in range(NCH):
                for hh in range(2):
                    for j in range(Gg):
                        is_last = (j == Gg - 1)
                        p.ev("tensor",
                             lambda e, ec=ec, hh=hh, j=j, g_i=g_i: e.matmul(
                                 E_ps[:, hh * 512:(hh + 1) * 512],
                                 b.wdB[:, j * C + ec * 128:j * C + (ec + 1) * 128],
                                 b.hsG[:, j * TC + hh * 512:j * TC + (hh + 1) * 512],
                                 start=(j == 0), stop=(j == Gg - 1)),
                             waits=(([("s_hevac", 2 * (2 * g_i + 2)),
                                      ("s_eadd", 32 * g_i + 2 * ec)] +
                                     [wdtok[2 * g_i + jj] for jj in range(Gg)])
                                    if (hh == 0 and j == 0) else []),
                             inc="s_emm" if is_last else None)
                    tgt = b.accS[:, ec * TC + hh * 512:ec * TC + (hh + 1) * 512]
                    if g_i == 0:
                        p.ev("vector", lambda e, t_=tgt, hh=hh: e.tensor_copy(
                            t_, E_ps[:, hh * 512:(hh + 1) * 512]),
                            waits=[("s_emm", 32 * g_i + 2 * ec + hh + 1)],
                            inc="s_eadd")
                    else:
                        p.ev("vector", lambda e, t_=tgt, hh=hh: e.tensor_tensor(
                            t_, t_, E_ps[:, hh * 512:(hh + 1) * 512], ALU.add),
                            waits=[("s_emm", 32 * g_i + 2 * ec + hh + 1)],
                            inc="s_eadd")
    w_ffn1 = ("s_eadd", 32 * 32)
    for ec in range(NCH):
        p.ev("scalar", lambda e, ec=ec: e.activation(
            b.gout[:, (ec % 2) * TC:(ec % 2 + 1) * TC],
            b.accS[:, ec * TC:(ec + 1) * TC], AF.Silu),
            waits=[w_ffn1] + ([b.stg.tok(ec - 2)] if ec >= 2 else []),
            inc="s_silu")
        b.stg.dma("gpsimd", b.g_t[ec],
                  b.gout[:, (ec % 2) * TC:(ec % 2 + 1) * TC],
                  waits=[("s_silu", ec + 1)])




_RUN = None


def _runner():
    return _RUN if _RUN is not None else run_bass_kernel_spmd


def host_masks(c):
    import ml_dtypes
    out = np.zeros((8, 128, 2048), dtype=np.float32)
    for si, g in enumerate(TILES[c]):
        blk = g // 16
        qpos = g * 128 + np.arange(128)[:, None]
        kpos = blk * 2048 + np.arange(2048)[None, :]
        out[si] = np.where(kpos <= qpos, 0.0, NEG)
    return out.astype(ml_dtypes.bfloat16)


def kernel(**inputs):
    run = _runner()
    x = np.ascontiguousarray(inputs["L0_attn_x"][0])
    ones = np.ones((128, 128), dtype=np.float32)
    idn = np.eye(128, dtype=np.float32)
    rmat = rmat_host()
    epsc = np.full((128, 1), 1e-5, dtype=np.float32)
    idxs = [core_token_idx(c) for c in range(NCORES)]
    ropes = [rope_tables_f(idxs[c]) for c in range(NCORES)]
    masks = [host_masks(c) for c in range(NCORES)]

    ncA0 = build_launch_A(do_rope=True)
    ncB0 = build_launch_B(0)
    ncA1 = build_launch_A(do_rope=False)
    ncB1 = build_launch_B(1)

    wpk0 = pack_lhsT(np.ascontiguousarray(inputs["L0_attn_Wqkv"].T))
    b2_0 = bias2d(inputs["L0_attn_bqkv"])
    w0pk = pack_lhsT(np.ascontiguousarray(inputs["L0_ffn_W0"].T))
    w1t = np.ascontiguousarray(inputs["L0_ffn_W1"].T)
    wot0 = np.ascontiguousarray(inputs["L0_attn_Wo"].T)
    wqkv1 = np.concatenate([inputs["L1_attn_Wq"], inputs["L1_attn_Wk"],
                            inputs["L1_attn_Wv"]], axis=1)
    wpk1 = pack_lhsT(np.ascontiguousarray(wqkv1))
    b2_1 = bias2d(np.concatenate([inputs["L1_attn_bq"], inputs["L1_attn_bk"],
                                  inputs["L1_attn_bv"]]))
    wot1 = np.ascontiguousarray(inputs["L1_attn_Wo"])
    wgpk = pack_lhsT(np.ascontiguousarray(inputs["L1_ffn_w_gate"]))
    wupk = pack_lhsT(np.ascontiguousarray(inputs["L1_ffn_w_up"]))
    wdt = np.ascontiguousarray(inputs["L1_ffn_w_down"])

    def launchA(nc, x_slices, wpk, b2, rope_on):
        in_maps = []
        for c in range(NCORES):
            cosf, sinf = ropes[c] if rope_on else (np.zeros((128, TC), np.float32),
                                                   np.zeros((128, TC), np.float32))
            in_maps.append(dict(xf=x_slices[c], wpk=wpk, b2=b2, cosf=cosf,
                                sinf=sinf, ones=ones, idn=idn, rmat=rmat,
                                epsc=epsc))
        res = run(nc, in_maps, core_ids=list(range(NCORES)))
        kfull = np.zeros((C, T), dtype=np.float32)
        vtfull = np.zeros((T, C), dtype=np.float32)
        for c in range(NCORES):
            kfull[:, idxs[c]] = res.results[c]["kf"]
            vtfull[idxs[c], :] = res.results[c]["vt"]
        return res, np.ascontiguousarray(kfull), np.ascontiguousarray(vtfull)

    x_slices = [np.ascontiguousarray(x[idxs[c]].T) for c in range(NCORES)]

    resA0, kfull0, vt0 = launchA(ncA0, x_slices, wpk0, b2_0, True)
    in_mapsB0 = []
    for c in range(NCORES):
        in_mapsB0.append(dict(
            xn=resA0.results[c]["xn"], xf=x_slices[c], qf=resA0.results[c]["qf"],
            kfull=kfull0, vtfull=vt0, wot=wot0, bo2=bias2d(inputs["L0_attn_bo"]),
            idn=idn, w0pk=w0pk, b0d=bias2d(inputs["L0_ffn_bias0"]),
            w1t=w1t, b1d=bias2d(inputs["L0_ffn_bias1"]), masks=masks[c]))
    resB0 = run(ncB0, in_mapsB0, core_ids=list(range(NCORES)))
    x1_slices = [np.ascontiguousarray(resB0.results[c]["xo"]) for c in range(NCORES)]

    resA1, kfull1, vt1 = launchA(ncA1, x1_slices, wpk1, b2_1, False)
    in_mapsB1 = []
    for c in range(NCORES):
        in_mapsB1.append(dict(
            xn=resA1.results[c]["xn"], xf=x1_slices[c], qf=resA1.results[c]["qf"],
            kfull=kfull1, vtfull=vt1, wot=wot1, bo2=bias2d(inputs["L1_attn_bo"]),
            idn=idn, wgpk=wgpk, wupk=wupk, wdt=wdt))
    resB1 = run(ncB1, in_mapsB1, core_ids=list(range(NCORES)))

    out = np.zeros((T, C), dtype=np.float32)
    for c in range(NCORES):
        out[idxs[c], :] = resB1.results[c]["xo"].T
    return out[None]
